# version 09
# baseline (speedup 1.0000x reference)
import numpy as np
import concourse.bass as bass
import concourse.mybir as mybir
from concourse.bass_utils import run_bass_kernel_spmd

F32 = mybir.dt.float32
BF16 = mybir.dt.bfloat16
I32 = mybir.dt.int32
AF = mybir.ActivationFunctionType
OP = mybir.AluOpType
PM = mybir.MatmulPerfMode.DoubleRow

H = 2048
HT = 16
TT = 512
NT = 5
TC = 2048
XW = 512 + TC
INTER = 5504
OT = 43
GUW = 2 * INTER
CW = 1024


NRING = 8
HAZ_WIN = 4

_IN_KEYS = ("in_", "in0", "in1", "data0", "data1", "initial", "scalar",
            "scalar1", "scalar2", "scale", "bias", "mul", "mask",
            "on_true", "on_false", "lhsT", "rhs", "weights")


def _ap_region(v):
    try:
        t = v.tensor
        name = t.name
        off = int(v.offset)
        span = 1
        for pair in v.ap:
            st, n = int(pair[0]), int(pair[1])
            span += st * (n - 1)
        return (name, off, off + span)
    except Exception:
        return None


class Ctx:

    def __init__(self, mode, handles=None, sems=None, ck=None):
        self.mode = mode
        self.handles = handles or {}
        self.sems = sems or {}
        self.ck = ck if ck is not None else {}
        self.cnt = {}
        for k in "TVS":
            self.cnt[k] = 0
        for q in "QG":
            for r in range(NRING):
                self.cnt[q + str(r)] = 0
        self.dmaj = {"Q": 0, "G": 0}
        self.lastw = {k: [] for k in "TVS"}
        self.hiwater = {}

    def op(self, eng, name, *args, inc=0, **kw):
        if name == "dma_start":
            j = self.dmaj[eng]
            self.dmaj[eng] = j + 1
            key = eng + str(j % NRING)
            if self.mode == "dry":
                self.cnt[key] = 16 * (j // NRING + 1)
                return
            if self.mode != eng:
                return
            if j >= NRING and 16 * (j // NRING) > self.hiwater.get(key, 0):
                self.hiwater[key] = 16 * (j // NRING)
                self.handles[eng].wait_ge(self.sems[key], 16 * (j // NRING))
            ins = getattr(self.handles[eng], name)(*args, **kw)
            ins.then_inc(self.sems[key], 16)
            return
        ins_regions = []
        out_regions = []
        if name == "memset":
            r = _ap_region(args[0])
            if r:
                out_regions.append(r)
        else:
            for k, v in kw.items():
                r = _ap_region(v)
                if r is None:
                    continue
                if k in ("out", "accum_out"):
                    out_regions.append(r)
                elif k in _IN_KEYS:
                    ins_regions.append(r)
        def _ovl(a, b):
            for (an, aa, ab) in a:
                for (bn, ba, bb) in b:
                    if an == bn and aa < bb and ba < ab:
                        return True
            return False

        need = 0
        for wval, wregs, rregs in self.lastw[eng]:
            if (_ovl(wregs, ins_regions) or _ovl(rregs, out_regions)
                    or _ovl(wregs, out_regions)):
                need = max(need, wval)
        myval = self.cnt[eng] + 1
        self.cnt[eng] = myval
        self.lastw[eng].append((myval, out_regions, ins_regions))
        if len(self.lastw[eng]) > HAZ_WIN:
            self.lastw[eng].pop(0)
        if self.mode == "dry":
            return
        if self.mode != eng:
            return
        if need and need > self.hiwater.get(eng, 0):
            self.hiwater[eng] = need
            self.handles[eng].wait_ge(self.sems[eng], need)
        ins = getattr(self.handles[eng], name)(*args, **kw)
        ins.then_inc(self.sems[eng], 1)

    def ckpt(self, name, eng):
        if self.mode == "dry":
            assert name not in self.ck, name
            if eng in ("Q", "G"):
                self.ck[name] = [(eng + str(r), self.cnt[eng + str(r)])
                                 for r in range(NRING)
                                 if self.cnt[eng + str(r)] > 0]
            else:
                self.ck[name] = [(eng, self.cnt[eng])]

    def wait(self, eng, *names):
        if self.mode == "dry":
            for n in names:
                assert n in self.ck, n
            return
        if self.mode != eng:
            return
        for n in names:
            for src, val in self.ck[n]:
                if val > 0 and val > self.hiwater.get(src, 0):
                    self.hiwater[src] = val
                    self.handles[eng].wait_ge(self.sems[src], val)


def build_program():
    nc = bass.Bass("TRN2", target_bir_lowering=False)

    x_h = nc.dram_tensor("x", [H, XW], F32, kind="ExternalInput")
    wk_h = nc.dram_tensor("wkT", [H, H], F32, kind="ExternalInput")
    wv_h = nc.dram_tensor("wvT", [H, H], F32, kind="ExternalInput")
    wr_h = nc.dram_tensor("wrT", [H, H], F32, kind="ExternalInput")
    wo_h = nc.dram_tensor("woT", [H, H], F32, kind="ExternalInput")
    giu_h = nc.dram_tensor("giuT", [H, GUW], I32, kind="ExternalInput")
    did_h = nc.dram_tensor("didT", [INTER, H], I32, kind="ExternalInput")
    cols_h = nc.dram_tensor("colsin", [128, 81], F32, kind="ExternalInput")
    lut_h = nc.dram_tensor("lutrow", [1, 16], F32, kind="ExternalInput")
    y_h = nc.dram_tensor("y", [H, TC], F32, kind="ExternalOutput")

    ckvr = nc.dram_tensor("ckvr", [H, 3 * H], BF16, kind="Internal")
    cwo = nc.dram_tensor("cwo", [HT * H, 128], BF16, kind="Internal")
    cwgu = nc.dram_tensor("cwgu", [H, GUW], BF16, kind="Internal")
    cwd = nc.dram_tensor("cwd", [INTER, H], BF16, kind="Internal")

    from contextlib import ExitStack
    es = ExitStack()

    def sbt(name, shape, dt):
        return es.enter_context(nc.sbuf_tensor(name, shape, dt))

    xe = sbt("xe", [128, HT * TT], F32)
    xnb = sbt("xnb", [128, HT * 513], BF16)
    pool = sbt("pool", [128, 30208], BF16)
    zb = sbt("zb", [128, 8192], BF16)
    wbuf = sbt("wbuf", [128, 17408], BF16)
    idx32 = sbt("idx32", [128, 2 * CW], I32)
    trans = sbt("trans", [128, 3074], F32)
    colsb = sbt("colsb", [128, 81], F32)
    colsd = sbt("colsd", [128, 48], F32)
    lutb = sbt("lutb", [128, 16], F32)
    nstate = sbt("nstate", [128, 16], F32)
    dstate = sbt("dstate", [128, 16], F32)
    onesrow = sbt("onesrow", [1, 128], F32)
    onesb = sbt("onesb", [128, 1], BF16)
    constsb = sbt("constsb", [128, 2], F32)
    nc.const_aps.register(F32, 1e-5, constsb[:, 0:1])
    nc.const_aps.register(F32, -1.0, constsb[:, 1:2])

    pb = [es.enter_context(nc.psum_tensor("pb%d" % i, [128, 512], F32))
          for i in range(8)]

    def kin(h):
        return pool[:, h * TT:(h + 1) * TT]

    def vin(h):
        return pool[:, 8192 + h * TT: 8192 + (h + 1) * TT]

    def rin(h):
        return pool[:, 16384 + h * TT: 16384 + (h + 1) * TT]

    def ybt(h):
        return pool[:, h * TT:(h + 1) * TT]

    def hact(o):
        return pool[:, 8192 + o * TT: 8192 + (o + 1) * TT]

    def xet(h):
        return xe[:, h * TT:(h + 1) * TT]

    def xnt(h):
        return xnb[:, h * 513 + 1: h * 513 + 513]

    def xnprev(h):
        return xnb[:, h * 513: h * 513 + 512]

    def zt(a):
        return zb[:, a * TT:(a + 1) * TT]

    def x2n(h):
        return zb[:, h * TT:(h + 1) * TT]

    ekv = trans[:, 0:512]
    ta = trans[:, 512:1024]
    ewb = trans[:, 1024:1536]
    ns = trans[:, 1536:2049]
    ds = trans[:, 2049:2562]
    ekb = trans[:, 2562:3074]

    ewc = colsd[:, 0:16]
    etfc = colsd[:, 16:32]
    ttmp = colsd[:, 32:48]
    tdc = colsb[:, 0:16]
    tfc = colsb[:, 16:32]
    tmkc = colsb[:, 32:48]
    tmvc = colsb[:, 48:64]
    tmrc = colsb[:, 64:80]
    maskc = colsb[:, 80:81]

    mu = ds[0:1, 0:512]
    msq = ta[0:1, 0:512]
    tsm = ewb[0:1, 0:512]
    rstd = ns[0:1, 0:512]

    def kvrbuf(p):
        return bass.AP(wbuf, p * 6144, [[17408, 128], [384, 16], [1, 384]])

    def kvrtile(p, h, m):
        return wbuf[:, p * 6144 + h * 384 + m * 128: p * 6144 + h * 384 + (m + 1) * 128]

    def ybuf_ap(p):
        return bass.AP(wbuf, 12288 + p * 2048, [[17408, 128], [128, 16], [1, 128]])

    def ybtile(p, a):
        return wbuf[:, 12288 + p * 2048 + a * 128: 12288 + p * 2048 + (a + 1) * 128]

    def gubuf_ap(p):
        return bass.AP(wbuf, p * 4096, [[17408, 128], [256, 16], [1, 256]])

    def gutile(p, h, m):
        return wbuf[:, p * 4096 + h * 256 + m * 128: p * 4096 + h * 256 + (m + 1) * 128]

    def dbuf_ap(p):
        return bass.AP(wbuf, p * 5504, [[17408, 128], [128, 43], [1, 128]])

    def dtile(p, o):
        return wbuf[:, p * 5504 + o * 128: p * 5504 + (o + 1) * 128]

    sqb = [wbuf[:, 16384:16896], wbuf[:, 16896:17408]]

    def _xp(piece, w2=512):
        return bass.AP(xnb, (2 * piece) * 513 + 1,
                       [[513 * HT, 128], [513, 2], [1, w2]])

    idxb2 = [_xp(0), _xp(1)]
    acc2 = _xp(2)
    accB = _xp(3)
    mrot = [_xp(4), _xp(5), _xp(6)]

    def xe_ap3():
        return bass.AP(xe, 0, [[HT * TT, 128], [TT, HT], [1, TT]])

    def x_hbm_ap(t):
        return bass.AP(x_h, TT * t, [[XW, 128], [128 * XW, HT], [1, TT]])

    def y_hbm_ap(t):
        return bass.AP(y_h, TT * t, [[TC, 128], [128 * TC, HT], [1, TT]])

    def script(ctx):
        W = ctx.wait
        O = ctx.op
        CK = ctx.ckpt

        O("Q", "dma_start", out=colsb[:, :], in_=cols_h[:, :], inc=16)
        O("Q", "dma_start", out=lutb[0:1, :], in_=lut_h[0:1, :], inc=16)
        CK("cols", "Q")
        O("V", "memset", onesrow[:, :], 1.0)
        O("V", "memset", onesb[:, :], 1.0)
        O("V", "memset", constsb[:, 0:1], 1e-5)
        O("V", "memset", constsb[:, 1:2], -1.0)
        O("V", "memset", nstate[:, :], 0.0)
        O("V", "memset", dstate[:, :], 0.0)
        O("V", "memset", bass.AP(xnb, 0, [[513 * HT, 128], [513, HT], [1, 1]]), 0.0,
          inc=1)
        CK("vinit", "V")
        W("T", "cols", "vinit")
        O("T", "matmul", out=pb[6][:, 0:16], lhsT=onesrow[:, :], rhs=lutb[0:1, :],
          start=True, stop=True, inc=1)
        CK("lutmm", "T")
        W("V", "lutmm")
        O("V", "tensor_copy", out=lutb[:, :], in_=pb[6][:, 0:16], inc=1)
        CK("lutb", "V")
        W("S", "cols", "vinit")
        O("S", "activation", out=ttmp, in_=tdc, func=AF.Exp, inc=1)
        CK("ttmpd", "S")
        W("S", "ttmpd")
        O("S", "activation", out=ewc, in_=ttmp, func=AF.Exp, scale=constsb[:, 1:2])
        O("S", "activation", out=etfc, in_=tfc, func=AF.Exp, inc=1)
        CK("chancols", "S")

        conv_jobs = []
        for m, wh in enumerate((wk_h, wv_h, wr_h)):
            for t in range(HT):
                conv_jobs.append(("kvr", m, t, wh))
        for t in range(HT):
            conv_jobs.append(("wo", 0, t, wo_h))
        for j, (kind, m, t, wh) in enumerate(conv_jobs):
            p = j % 2
            src32 = xe[:, p * 2048:(p + 1) * 2048]
            dst16 = wbuf[:, p * 2048:(p + 1) * 2048]
            if j >= 2:
                W("G", "convc%d" % (j - 2))
                W("G", "convs%d" % (j - 2))
            O("G", "dma_start", out=src32,
              in_=bass.AP(wh, t * 128 * H, [[H, 128], [1, 2048]]), inc=16)
            CK("convl%d" % j, "G")
            W("S", "convl%d" % j)
            O("S", "activation", out=dst16, in_=src32, func=AF.Copy, inc=1)
            CK("convc%d" % j, "S")
            W("G", "convc%d" % j)
            if kind == "kvr":
                dap = bass.AP(ckvr, (t * 128) * (3 * H) + m * 128,
                              [[3 * H, 128], [384, 16], [1, 128]])
            else:
                dap = bass.AP(cwo, (t * 128) * 128,
                              [[128, 128], [H * 128, 16], [1, 128]])
            O("G", "dma_start", out=dap,
              in_=bass.AP(wbuf, p * 2048, [[17408, 128], [128, 16], [1, 128]]),
              inc=16)
            CK("convs%d" % j, "G")
        ck_conv_done = "convs%d" % (len(conv_jobs) - 1)
        ck_conv_cast_done = "convc%d" % (len(conv_jobs) - 1)

        exp_jobs = []
        for t in range(HT):
            c0 = 0
            while c0 < GUW:
                cw = min(CW, GUW - c0)
                exp_jobs.append((giu_h, cwgu, t, c0, cw, GUW))
                c0 += cw
        for t in range(OT):
            c0 = 0
            while c0 < H:
                cw = min(CW, H - c0)
                exp_jobs.append((did_h, cwd, t, c0, cw, H))
                c0 += cw
        exp_state = {"j": 0}

        def exp_chunk():
            j = exp_state["j"]
            if j >= len(exp_jobs):
                return False
            exp_state["j"] = j + 1
            srch, dsth, rt, c0, cw, rowp = exp_jobs[j]
            w2 = cw // 2
            p = j % 2
            if j >= 2:
                W("Q", "expc%d" % (j - 2))
            O("Q", "dma_start", out=idx32[:, p * CW:p * CW + cw],
              in_=bass.AP(srch, rt * 128 * rowp + c0, [[rowp, 128], [1, cw]]),
              inc=16)
            CK("expl%d" % j, "Q")
            W("S", "expl%d" % j)
            if j >= 2:
                W("S", "expu%d" % (j - 2))
            O("S", "activation", out=_xp(p, w2),
              in_=idx32[:, p * CW:p * CW + cw], func=AF.Copy, inc=1)
            CK("expc%d" % j, "S")
            W("V", "lutb", "expc%d" % j)
            if j >= 1:
                W("V", "expd%d" % (j - 1))
            ib = idxb2[p]

            def _m(l, dst):
                O("V", "tensor_scalar", out=dst, in0=ib,
                  scalar1=float(l), scalar2=lutb[:, l:l + 1],
                  op0=OP.is_equal, op1=OP.mult)

            def _add(dst, src):
                O("V", "tensor_add", out=dst, in0=dst, in1=src)

            _m(0, acc2)
            _m(1, accB)
            _m(2, mrot[0])
            _m(3, mrot[1])
            _m(4, mrot[2])
            accs = [acc2, accB]
            for l in range(5, 16):
                _add(accs[l % 2], mrot[(l - 5) % 3])
                _m(l, mrot[(l - 5) % 3])
            _add(accs[0], mrot[2])
            _add(accs[1], mrot[0])
            _add(accs[0], mrot[1])
            O("V", "tensor_add", out=acc2, in0=acc2, in1=accB, inc=1)
            CK("expu%d" % j, "V")
            W("Q", "expu%d" % j)
            O("Q", "dma_start",
              out=bass.AP(dsth, rt * 128 * rowp + c0,
                          [[rowp, 128], [w2, 2], [1, w2]]),
              in_=_xp(2, w2), inc=16)
            CK("expd%d" % j, "Q")
            return True

        ck_exp_done = "expd%d" % (len(exp_jobs) - 1)
        ck_exp_udone = "expu%d" % (len(exp_jobs) - 1)

        W("Q", ck_conv_cast_done)
        O("Q", "dma_start", out=xe_ap3(), in_=x_hbm_ap(0), inc=16)
        CK("xl0", "Q")

        W("G", ck_conv_done)

        for i in range(NT):
            warm = (i == 0)

            if i == 0:
                wS = ["xl0"]
                wT = ["lutb"]
            elif i == 1:
                wS = ["xl1", "vmm0_15"]
                wT = []
            else:
                wS = ["xl%d" % i, "outv%d" % (i - 1)]
                wT = ["outv%d" % (i - 1)]
            W("S", *wS)
            for h in range(HT):
                O("S", "activation", out=kin(h), in_=xet(h), func=AF.Copy)
                O("S", "activation", out=vin(h), in_=xet(h), func=AF.Square,
                  inc=(1 if h == HT - 1 else 0))
            CK("acast%d" % i, "S")
            W("T", *(wT + ["acast%d" % i]))
            for h in range(HT):
                O("T", "matmul", out=pb[6][0:1, :], lhsT=onesb[:, :], rhs=kin(h),
                  start=(h == 0), stop=(h == HT - 1))
                O("T", "matmul", out=pb[7][0:1, :], lhsT=onesb[:, :], rhs=vin(h),
                  start=(h == 0), stop=(h == HT - 1),
                  inc=(1 if h == HT - 1 else 0))
            CK("astat%d" % i, "T")
            W("V", "astat%d" % i)
            O("V", "tensor_scalar", out=mu, in0=pb[6][0:1, :], scalar1=1.0 / H,
              scalar2=None, op0=OP.mult)
            O("V", "tensor_scalar", out=msq, in0=pb[7][0:1, :], scalar1=1.0 / H,
              scalar2=None, op0=OP.mult)
            O("V", "tensor_mul", out=tsm, in0=mu, in1=mu)
            O("V", "tensor_sub", out=msq, in0=msq, in1=tsm, inc=1)
            CK("avar%d" % i, "V")
            W("S", "avar%d" % i)
            O("S", "activation", out=rstd, in_=msq, func=AF.Sqrt, bias=constsb[0:1, 0:1], inc=1)
            CK("asqr%d" % i, "S")
            W("V", "asqr%d" % i)
            O("V", "reciprocal", out=rstd, in_=rstd, inc=1)
            CK("arstd%d" % i, "V")
            W("T", "arstd%d" % i)
            O("T", "matmul", out=pb[6][:, :], lhsT=onesrow[:, :], rhs=mu,
              start=True, stop=True)
            O("T", "matmul", out=pb[7][:, :], lhsT=onesrow[:, :], rhs=rstd,
              start=True, stop=True, inc=1)
            CK("abc%d" % i, "T")
            W("V", "abc%d" % i)
            for h in range(HT):
                O("V", "tensor_sub", out=ta, in0=xet(h), in1=pb[6][:, :])
                O("V", "tensor_mul", out=xnt(h), in0=ta, in1=pb[7][:, :],
                  inc=(1 if h == HT - 1 else 0))
            CK("axn%d" % i, "V")
            for h in range(HT):
                O("V", "tensor_sub", out=ta, in0=xnt(h), in1=xnprev(h))
                O("V", "scalar_tensor_tensor", out=kin(h), in0=ta,
                  scalar=tmkc[:, h:h + 1], in1=xnprev(h), op0=OP.mult, op1=OP.add)
                O("V", "scalar_tensor_tensor", out=vin(h), in0=ta,
                  scalar=tmvc[:, h:h + 1], in1=xnprev(h), op0=OP.mult, op1=OP.add)
                if not warm:
                    O("V", "scalar_tensor_tensor", out=rin(h), in0=ta,
                      scalar=tmrc[:, h:h + 1], in1=xnprev(h), op0=OP.mult,
                      op1=OP.add)
            O("V", "tensor_copy",
              out=bass.AP(xnb, 0, [[513 * HT, 128], [513, HT], [1, 1]]),
              in_=bass.AP(xnb, 512, [[513 * HT, 128], [513, HT], [1, 1]]), inc=1)
            CK("amix%d" % i, "V")

            for a in range(HT):
                pk = a % 2
                if a >= 2:
                    W("G", ("vmm%d_%d" if warm else "rmm%d_%d") % (i, a - 2))
                elif a == 0 and i == 1:
                    W("G", "vmm0_15")
                elif a == 0 and i >= 2:
                    W("G", "dmm%d_15" % (i - 1))
                O("G", "dma_start", out=kvrbuf(pk),
                  in_=bass.AP(ckvr, a * 384,
                              [[3 * H, 128], [128 * 3 * H, 16], [1, 384]]),
                  inc=16)
                CK("wkvr%d_%d" % (i, a), "G")
                if a == 0:
                    W("T", "amix%d" % i)
                if a >= 2:
                    W("T", ("scan%d_%d" if warm else "z%d_%d") % (i, a - 2))
                W("T", "wkvr%d_%d" % (i, a))
                for h in range(HT):
                    O("T", "matmul", out=pb[0 + pk][:, :], lhsT=kvrtile(pk, h, 0),
                      rhs=kin(h), start=(h == 0), stop=(h == HT - 1),
                      inc=(1 if h == HT - 1 else 0))
                CK("kmm%d_%d" % (i, a), "T")
                for h in range(HT):
                    O("T", "matmul", out=pb[2 + pk][:, :], lhsT=kvrtile(pk, h, 1),
                      rhs=vin(h), start=(h == 0), stop=(h == HT - 1),
                      inc=(1 if h == HT - 1 else 0))
                CK("vmm%d_%d" % (i, a), "T")
                if not warm:
                    for h in range(HT):
                        O("T", "matmul", out=pb[4 + pk][:, :],
                          lhsT=kvrtile(pk, h, 2), rhs=rin(h),
                          start=(h == 0), stop=(h == HT - 1),
                          inc=(1 if h == HT - 1 else 0))
                    CK("rmm%d_%d" % (i, a), "T")
                W("S", "kmm%d_%d" % (i, a))
                if a >= 1:
                    W("S", "scan%d_%d" % (i, a - 1))
                O("S", "activation", out=ekb, in_=pb[0 + pk][:, :],
                  func=AF.Exp, inc=1)
                CK("ekx%d_%d" % (i, a), "S")
                if warm:
                    W("S", "ekx%d_%d" % (i, a))
                    O("S", "mul", out=ekb, in_=ekb, mul=maskc, inc=1)
                CK("ek%d_%d" % (i, a), "S")
                if not warm:
                    W("S", "rmm%d_%d" % (i, a))
                    O("S", "activation", out=pb[4 + pk][:, :], in_=pb[4 + pk][:, :],
                      func=AF.Sigmoid, inc=1)
                    CK("sig%d_%d" % (i, a), "S")
                O("V", "memset", ewb, 1.0)
                O("V", "tensor_scalar", out=ewb, in0=ewb,
                  scalar1=ewc[:, a:a + 1], scalar2=None, op0=OP.mult)
                W("V", "ek%d_%d" % (i, a), "vmm%d_%d" % (i, a))
                O("V", "tensor_mul", out=ekv, in0=ekb,
                  in1=pb[2 + pk][:, :])
                O("V", "tensor_copy", out=ns[:, 0:1], in_=nstate[:, a:a + 1])
                O("V", "tensor_copy", out=ds[:, 0:1], in_=dstate[:, a:a + 1])
                O("V", "tensor_tensor_scan", out=ns[:, 1:513], data0=ewb,
                  data1=ekv, initial=nstate[:, a:a + 1], op0=OP.mult, op1=OP.add)
                O("V", "tensor_tensor_scan", out=ds[:, 1:513], data0=ewb,
                  data1=ekb, initial=dstate[:, a:a + 1],
                  op0=OP.mult, op1=OP.add)
                O("V", "tensor_copy", out=nstate[:, a:a + 1], in_=ns[:, 512:513])
                O("V", "tensor_copy", out=dstate[:, a:a + 1], in_=ds[:, 512:513],
                  inc=1)
                CK("scan%d_%d" % (i, a), "V")
                if warm:
                    exp_chunk()
                    exp_chunk()
                    continue
                W("S", "scan%d_%d" % (i, a))
                O("S", "activation", out=ekv, in_=ekb, func=AF.Copy,
                  scale=etfc[:, a:a + 1], inc=1)
                CK("ektf%d_%d" % (i, a), "S")
                W("V", "ektf%d_%d" % (i, a))
                O("V", "tensor_mul", out=ta, in0=ekv, in1=pb[2 + pk][:, :])
                O("V", "tensor_add", out=ta, in0=ta, in1=ns[:, 0:512])
                O("V", "tensor_add", out=ns[:, 0:512], in0=ekv, in1=ds[:, 0:512])
                O("V", "reciprocal", out=ns[:, 0:512], in_=ns[:, 0:512])
                W("V", "sig%d_%d" % (i, a))
                O("V", "tensor_mul", out=ta, in0=ta, in1=ns[:, 0:512])
                O("V", "tensor_mul", out=zt(a), in0=ta, in1=pb[4 + pk][:, :],
                  inc=1)
                CK("z%d_%d" % (i, a), "V")
                if i == 1:
                    exp_chunk()
                    exp_chunk()

            if warm:
                W("Q", "axn%d" % i)
                O("Q", "dma_start", out=xe_ap3(), in_=x_hbm_ap(i + 1), inc=16)
                CK("xl%d" % (i + 1), "Q")
                continue

            if i == 1:
                while exp_chunk():
                    pass
            for h in range(HT):
                ph = h % 2
                if h >= 2:
                    W("G", "ymm%d_%d" % (i, h - 2))
                O("G", "dma_start", out=ybuf_ap(ph),
                  in_=bass.AP(cwo, h * H * 128,
                              [[128, 128], [128 * 128, 16], [1, 128]]), inc=16)
                CK("wy%d_%d" % (i, h), "G")
                if h == 0:
                    W("T", "z%d_15" % i)
                if h >= 2:
                    W("T", "ybc%d_%d" % (i, h - 2))
                W("T", "wy%d_%d" % (i, h))
                for a in range(HT):
                    O("T", "matmul", out=pb[0 + ph][:, :], lhsT=ybtile(ph, a),
                      rhs=zt(a), start=(a == 0), stop=(a == HT - 1),
                      inc=(1 if a == HT - 1 else 0))
                CK("ymm%d_%d" % (i, h), "T")
                W("S", "ymm%d_%d" % (i, h))
                O("S", "activation", out=ybt(h), in_=pb[0 + ph][:, :], func=AF.Copy,
                  inc=1)
                CK("ybc%d_%d" % (i, h), "S")

            for h in range(HT):
                W("S", "ybc%d_%d" % (i, h))
                if h >= 2:
                    W("S", "bsm%d_%d" % (i, h - 2))
                O("S", "activation", out=sqb[h % 2], in_=ybt(h), func=AF.Square,
                  inc=1)
                CK("bsq%d_%d" % (i, h), "S")
                W("T", "bsq%d_%d" % (i, h))
                O("T", "matmul", out=pb[6][0:1, :], lhsT=onesb[:, :], rhs=ybt(h),
                  start=(h == 0), stop=(h == HT - 1))
                O("T", "matmul", out=pb[7][0:1, :], lhsT=onesb[:, :],
                  rhs=sqb[h % 2], start=(h == 0), stop=(h == HT - 1), inc=1)
                CK("bsm%d_%d" % (i, h), "T")
            CK("bstat%d" % i, "T")
            W("V", "bstat%d" % i)
            O("V", "tensor_scalar", out=mu, in0=pb[6][0:1, :], scalar1=1.0 / H,
              scalar2=None, op0=OP.mult)
            O("V", "tensor_scalar", out=msq, in0=pb[7][0:1, :], scalar1=1.0 / H,
              scalar2=None, op0=OP.mult)
            O("V", "tensor_mul", out=tsm, in0=mu, in1=mu)
            O("V", "tensor_sub", out=msq, in0=msq, in1=tsm, inc=1)
            CK("bvar%d" % i, "V")
            W("S", "bvar%d" % i)
            O("S", "activation", out=rstd, in_=msq, func=AF.Sqrt, bias=constsb[0:1, 0:1], inc=1)
            CK("bsqr%d" % i, "S")
            W("V", "bsqr%d" % i)
            O("V", "reciprocal", out=rstd, in_=rstd, inc=1)
            CK("brstd%d" % i, "V")
            W("T", "brstd%d" % i)
            O("T", "matmul", out=pb[6][:, :], lhsT=onesrow[:, :], rhs=mu,
              start=True, stop=True)
            O("T", "matmul", out=pb[7][:, :], lhsT=onesrow[:, :], rhs=rstd,
              start=True, stop=True, inc=1)
            CK("bbc%d" % i, "T")
            W("V", "bbc%d" % i)
            for h in range(HT):
                O("V", "tensor_sub", out=ta, in0=ybt(h), in1=pb[6][:, :])
                O("V", "tensor_mul", out=ta, in0=ta, in1=pb[7][:, :])
                O("V", "tensor_add", out=xet(h), in0=xet(h), in1=ta,
                  inc=(1 if h == HT - 1 else 0))
            CK("x1%d" % i, "V")

            for h in range(HT):
                if h == 0:
                    W("S", "x1%d" % i)
                if h >= 2:
                    W("S", "csm%d_%d" % (i, h - 2))
                O("S", "activation", out=rin(h), in_=xet(h), func=AF.Copy)
                O("S", "activation", out=sqb[h % 2], in_=xet(h), func=AF.Square,
                  inc=1)
                CK("csq%d_%d" % (i, h), "S")
                if h == 0:
                    W("T", "x1%d" % i)
                W("T", "csq%d_%d" % (i, h))
                O("T", "matmul", out=pb[6][0:1, :], lhsT=onesb[:, :], rhs=rin(h),
                  start=(h == 0), stop=(h == HT - 1))
                O("T", "matmul", out=pb[7][0:1, :], lhsT=onesb[:, :],
                  rhs=sqb[h % 2], start=(h == 0), stop=(h == HT - 1), inc=1)
                CK("csm%d_%d" % (i, h), "T")
            CK("cstat%d" % i, "T")
            W("V", "cstat%d" % i)
            O("V", "tensor_scalar", out=mu, in0=pb[6][0:1, :], scalar1=1.0 / H,
              scalar2=None, op0=OP.mult)
            O("V", "tensor_scalar", out=msq, in0=pb[7][0:1, :], scalar1=1.0 / H,
              scalar2=None, op0=OP.mult)
            O("V", "tensor_mul", out=tsm, in0=mu, in1=mu)
            O("V", "tensor_sub", out=msq, in0=msq, in1=tsm, inc=1)
            CK("cvar%d" % i, "V")
            W("S", "cvar%d" % i)
            O("S", "activation", out=rstd, in_=msq, func=AF.Sqrt, bias=constsb[0:1, 0:1], inc=1)
            CK("csqr%d" % i, "S")
            W("V", "csqr%d" % i)
            O("V", "reciprocal", out=rstd, in_=rstd, inc=1)
            CK("crstd%d" % i, "V")
            W("T", "crstd%d" % i)
            O("T", "matmul", out=pb[6][:, :], lhsT=onesrow[:, :], rhs=mu,
              start=True, stop=True)
            O("T", "matmul", out=pb[7][:, :], lhsT=onesrow[:, :], rhs=rstd,
              start=True, stop=True, inc=1)
            CK("cbc%d" % i, "T")
            W("V", "cbc%d" % i)
            for h in range(HT):
                O("V", "tensor_sub", out=ta, in0=xet(h), in1=pb[6][:, :])
                O("V", "tensor_mul", out=x2n(h), in0=ta, in1=pb[7][:, :],
                  inc=(1 if h == HT - 1 else 0))
            CK("x2n%d" % i, "V")

            for o in range(OT):
                po = o % 2
                if o < 2:
                    W("Q", ck_exp_done, "rmm%d_15" % i)
                else:
                    W("Q", "gumm%d_%d" % (i, o - 2))
                O("Q", "dma_start", out=gubuf_ap(po),
                  in_=bass.AP(cwgu, o * 256,
                              [[GUW, 128], [128 * GUW, 16], [1, 256]]), inc=16)
                CK("wgu%d_%d" % (i, o), "Q")
                if o == 0:
                    W("T", "x2n%d" % i)
                if o >= 2:
                    W("T", "hact%d_%d" % (i, o - 2))
                W("T", "wgu%d_%d" % (i, o))
                for h in range(HT):
                    O("T", "matmul", out=pb[0 + po][:, :], lhsT=gutile(po, h, 0),
                      rhs=x2n(h), start=(h == 0), stop=(h == HT - 1),
                      inc=(1 if h == HT - 1 else 0))
                CK("gmm%d_%d" % (i, o), "T")
                for h in range(HT):
                    O("T", "matmul", out=pb[2 + po][:, :], lhsT=gutile(po, h, 1),
                      rhs=x2n(h), start=(h == 0), stop=(h == HT - 1),
                      inc=(1 if h == HT - 1 else 0))
                CK("gumm%d_%d" % (i, o), "T")
                W("S", "gmm%d_%d" % (i, o))
                O("S", "activation", out=sqb[o % 2], in_=pb[0 + po][:, :],
                  func=AF.Silu, inc=1)
                CK("silu%d_%d" % (i, o), "S")
                W("V", "silu%d_%d" % (i, o), "gumm%d_%d" % (i, o))
                O("V", "tensor_mul", out=hact(o), in0=sqb[o % 2],
                  in1=pb[2 + po][:, :], inc=1)
                CK("hact%d_%d" % (i, o), "V")

            for h in range(HT):
                ph = h % 2
                if h == 0:
                    W("G", "gumm%d_%d" % (i, OT - 1))
                if h >= 2:
                    W("G", "dmm%d_%d" % (i, h - 2))
                O("G", "dma_start", out=dbuf_ap(ph),
                  in_=bass.AP(cwd, h * 128, [[H, 128], [128 * H, 43], [1, 128]]),
                  inc=16)
                CK("wd%d_%d" % (i, h), "G")
                if h == 0:
                    W("T", "hact%d_42" % i)
                if h >= 2:
                    W("T", "y2c%d_%d" % (i, h - 2))
                W("T", "wd%d_%d" % (i, h))
                for o in range(OT):
                    O("T", "matmul", out=pb[4 + ph][:, :], lhsT=dtile(ph, o),
                      rhs=hact(o), start=(o == 0), stop=(o == OT - 1),
                      inc=(1 if o == OT - 1 else 0))
                CK("dmm%d_%d" % (i, h), "T")
                W("S", "dmm%d_%d" % (i, h))
                O("S", "activation", out=ybt(h), in_=pb[4 + ph][:, :],
                  func=AF.Copy, inc=1)
                CK("y2c%d_%d" % (i, h), "S")

            for h in range(HT):
                W("S", "y2c%d_%d" % (i, h))
                if h >= 2:
                    W("S", "dsm%d_%d" % (i, h - 2))
                O("S", "activation", out=sqb[h % 2], in_=ybt(h), func=AF.Square,
                  inc=1)
                CK("dsq%d_%d" % (i, h), "S")
                if h == 0:
                    W("T", "x2n%d" % i)
                W("T", "dsq%d_%d" % (i, h))
                O("T", "matmul", out=pb[6][0:1, :], lhsT=onesb[:, :], rhs=ybt(h),
                  start=(h == 0), stop=(h == HT - 1))
                O("T", "matmul", out=pb[7][0:1, :], lhsT=onesb[:, :],
                  rhs=sqb[h % 2], start=(h == 0), stop=(h == HT - 1), inc=1)
                CK("dsm%d_%d" % (i, h), "T")
            CK("dstat%d" % i, "T")
            W("V", "dstat%d" % i)
            O("V", "tensor_scalar", out=mu, in0=pb[6][0:1, :], scalar1=1.0 / H,
              scalar2=None, op0=OP.mult)
            O("V", "tensor_scalar", out=msq, in0=pb[7][0:1, :], scalar1=1.0 / H,
              scalar2=None, op0=OP.mult)
            O("V", "tensor_mul", out=tsm, in0=mu, in1=mu)
            O("V", "tensor_sub", out=msq, in0=msq, in1=tsm, inc=1)
            CK("dvar%d" % i, "V")
            W("S", "dvar%d" % i)
            O("S", "activation", out=rstd, in_=msq, func=AF.Sqrt, bias=constsb[0:1, 0:1], inc=1)
            CK("dsqr%d" % i, "S")
            W("V", "dsqr%d" % i)
            O("V", "reciprocal", out=rstd, in_=rstd, inc=1)
            CK("drstd%d" % i, "V")
            W("T", "drstd%d" % i)
            O("T", "matmul", out=pb[6][:, :], lhsT=onesrow[:, :], rhs=mu,
              start=True, stop=True)
            O("T", "matmul", out=pb[7][:, :], lhsT=onesrow[:, :], rhs=rstd,
              start=True, stop=True, inc=1)
            CK("dbc%d" % i, "T")
            W("V", "dbc%d" % i)
            for h in range(HT):
                O("V", "tensor_sub", out=ta, in0=ybt(h), in1=pb[6][:, :])
                O("V", "tensor_mul", out=ta, in0=ta, in1=pb[7][:, :])
                O("V", "tensor_add", out=xet(h), in0=xet(h), in1=ta,
                  inc=(1 if h == HT - 1 else 0))
            CK("outv%d" % i, "V")

            W("Q", "outv%d" % i)
            O("Q", "dma_start", out=y_hbm_ap(i - 1), in_=xe_ap3(), inc=16)
            CK("outd%d" % i, "Q")
            if i + 1 < NT:
                W("Q", "outd%d" % i)
                O("Q", "dma_start", out=xe_ap3(), in_=x_hbm_ap(i + 1), inc=16)
                CK("xl%d" % (i + 1), "Q")

        if ctx.mode == "dry":
            ctx.ck["END_Q"] = [("Q" + str(r), ctx.cnt["Q" + str(r)])
                               for r in range(NRING)
                               if ctx.cnt["Q" + str(r)] > 0]
        if ctx.mode == "Q":
            for src, val in ctx.ck["END_Q"]:
                ctx.handles["Q"].wait_ge(ctx.sems[src], val)

    dry = Ctx("dry")
    script(dry)

    with es:
        with ExitStack() as ses:
            sems = {}
            for k in "TVS":
                sems[k] = ses.enter_context(nc.semaphore("s" + k))
            for q in "QG":
                for r in range(NRING):
                    sems[q + str(r)] = ses.enter_context(
                        nc.semaphore("s%s%d" % (q, r)))
            block = ses.enter_context(nc.Block())

            @block.tensor
            def _(eng):
                script(Ctx("T", {"T": eng}, sems, dry.ck))

            @block.vector
            def _(eng):
                script(Ctx("V", {"V": eng}, sems, dry.ck))

            @block.scalar
            def _(eng):
                script(Ctx("S", {"S": eng}, sems, dry.ck))

            @block.sync
            def _(eng):
                script(Ctx("Q", {"Q": eng}, sems, dry.ck))

            @block.gpsimd
            def _(eng):
                script(Ctx("G", {"G": eng}, sems, dry.ck))

    return nc


def kernel(L0_attn_hidden, L0_attn_time_decay, L0_attn_time_first,
           L0_attn_time_mix_key, L0_attn_time_mix_value,
           L0_attn_time_mix_receptance, L0_attn_W_key, L0_attn_W_value,
           L0_attn_W_receptance, L0_attn_W_output, L0_ffn_lut,
           L0_ffn_gate_idx, L0_ffn_up_idx, L0_ffn_down_idx):
    B, T, _ = L0_attn_hidden.shape

    wkT = np.ascontiguousarray(np.asarray(L0_attn_W_key, np.float32).T)
    wvT = np.ascontiguousarray(np.asarray(L0_attn_W_value, np.float32).T)
    wrT = np.ascontiguousarray(np.asarray(L0_attn_W_receptance, np.float32).T)
    woT = np.ascontiguousarray(np.asarray(L0_attn_W_output, np.float32).T)

    g = np.asarray(L0_ffn_gate_idx, np.int32).T.reshape(H, OT, 128)
    u = np.asarray(L0_ffn_up_idx, np.int32).T.reshape(H, OT, 128)
    giuT = np.ascontiguousarray(np.stack([g, u], axis=2).reshape(H, GUW))
    didT = np.ascontiguousarray(np.asarray(L0_ffn_down_idx, np.int32).T)

    lutrow = np.ascontiguousarray(
        np.asarray(L0_ffn_lut, np.float32).reshape(1, 16))

    def colsify(v):
        return np.asarray(v, np.float32).reshape(HT, 128).T

    x_full = np.asarray(L0_attn_hidden, np.float32)

    in_maps = []
    for c in range(8):
        b, q = c // 4, c % 4
        t0 = q * TC
        xs = np.zeros((XW, H), np.float32)
        if q == 0:
            xs[512:] = x_full[b, 0:TC]
        else:
            xs[:] = x_full[b, t0 - 512:t0 + TC]
        x_fm = np.ascontiguousarray(xs.T)

        cols = np.zeros((128, 81), np.float32)
        cols[:, 0:16] = colsify(L0_attn_time_decay)
        cols[:, 16:32] = colsify(L0_attn_time_first)
        cols[:, 32:48] = colsify(np.asarray(L0_attn_time_mix_key).reshape(-1))
        cols[:, 48:64] = colsify(np.asarray(L0_attn_time_mix_value).reshape(-1))
        cols[:, 64:80] = colsify(
            np.asarray(L0_attn_time_mix_receptance).reshape(-1))
        cols[:, 80] = 0.0 if q == 0 else 1.0

        in_maps.append({
            "x": x_fm, "wkT": wkT, "wvT": wvT, "wrT": wrT, "woT": woT,
            "giuT": giuT, "didT": didT, "colsin": cols, "lutrow": lutrow,
        })

    nc = build_program()
    res = run_bass_kernel_spmd(nc, in_maps, core_ids=list(range(8)))

    out = np.empty((B, T, H), np.float32)
    for c in range(8):
        b, q = c // 4, c % 4
        out[b, q * TC:(q + 1) * TC, :] = res.results[c]["y"].T
    return out


# version 10
# speedup vs baseline: 1.0685x; 1.0685x over previous
import numpy as np
import concourse.bass as bass
import concourse.mybir as mybir
from concourse.bass_utils import run_bass_kernel_spmd

F32 = mybir.dt.float32
BF16 = mybir.dt.bfloat16
I32 = mybir.dt.int32
AF = mybir.ActivationFunctionType
OP = mybir.AluOpType
PM = mybir.MatmulPerfMode.DoubleRow

H = 2048
HT = 16
TT = 512
NT = 5
TC = 2048
XW = 512 + TC
INTER = 5504
OT = 43
GUW = 2 * INTER
CW = 2048


NRING = 8
HAZ_WIN = 4

_IN_KEYS = ("in_", "in0", "in1", "data0", "data1", "initial", "scalar",
            "scalar1", "scalar2", "scale", "bias", "mul", "mask",
            "on_true", "on_false", "lhsT", "rhs", "weights")


def _ap_region(v):
    try:
        t = v.tensor
        name = t.name
        off = int(v.offset)
        span = 1
        for pair in v.ap:
            st, n = int(pair[0]), int(pair[1])
            span += st * (n - 1)
        return (name, off, off + span)
    except Exception:
        return None


class Ctx:

    def __init__(self, mode, handles=None, sems=None, ck=None):
        self.mode = mode
        self.handles = handles or {}
        self.sems = sems or {}
        self.ck = ck if ck is not None else {}
        self.cnt = {}
        for k in "TVS":
            self.cnt[k] = 0
        for q in "QG":
            for r in range(NRING):
                self.cnt[q + str(r)] = 0
        self.dmaj = {"Q": 0, "G": 0}
        self.lastw = {k: [] for k in "TVS"}
        self.hiwater = {}

    def op(self, eng, name, *args, inc=0, **kw):
        if name == "dma_start":
            j = self.dmaj[eng]
            self.dmaj[eng] = j + 1
            key = eng + str(j % NRING)
            if self.mode == "dry":
                self.cnt[key] = 16 * (j // NRING + 1)
                return
            if self.mode != eng:
                return
            if j >= NRING and 16 * (j // NRING) > self.hiwater.get(key, 0):
                self.hiwater[key] = 16 * (j // NRING)
                self.handles[eng].wait_ge(self.sems[key], 16 * (j // NRING))
            ins = getattr(self.handles[eng], name)(*args, **kw)
            ins.then_inc(self.sems[key], 16)
            return
        ins_regions = []
        out_regions = []
        if name == "memset":
            r = _ap_region(args[0])
            if r:
                out_regions.append(r)
        else:
            for k, v in kw.items():
                r = _ap_region(v)
                if r is None:
                    continue
                if k in ("out", "accum_out"):
                    out_regions.append(r)
                elif k in _IN_KEYS:
                    ins_regions.append(r)
        def _ovl(a, b):
            for (an, aa, ab) in a:
                for (bn, ba, bb) in b:
                    if an == bn and aa < bb and ba < ab:
                        return True
            return False

        need = 0
        for wval, wregs, rregs in self.lastw[eng]:
            if (_ovl(wregs, ins_regions) or _ovl(rregs, out_regions)
                    or _ovl(wregs, out_regions)):
                need = max(need, wval)
        myval = self.cnt[eng] + 1
        self.cnt[eng] = myval
        self.lastw[eng].append((myval, out_regions, ins_regions))
        if len(self.lastw[eng]) > HAZ_WIN:
            self.lastw[eng].pop(0)
        if self.mode == "dry":
            return
        if self.mode != eng:
            return
        if need and need > self.hiwater.get(eng, 0):
            self.hiwater[eng] = need
            self.handles[eng].wait_ge(self.sems[eng], need)
        ins = getattr(self.handles[eng], name)(*args, **kw)
        ins.then_inc(self.sems[eng], 1)

    def ckpt(self, name, eng):
        if self.mode == "dry":
            assert name not in self.ck, name
            if eng in ("Q", "G"):
                self.ck[name] = [(eng + str(r), self.cnt[eng + str(r)])
                                 for r in range(NRING)
                                 if self.cnt[eng + str(r)] > 0]
            else:
                self.ck[name] = [(eng, self.cnt[eng])]

    def wait(self, eng, *names):
        if self.mode == "dry":
            for n in names:
                assert n in self.ck, n
            return
        if self.mode != eng:
            return
        for n in names:
            for src, val in self.ck[n]:
                if val > 0 and val > self.hiwater.get(src, 0):
                    self.hiwater[src] = val
                    self.handles[eng].wait_ge(self.sems[src], val)


def build_program():
    nc = bass.Bass("TRN2", target_bir_lowering=False)

    x_h = nc.dram_tensor("x", [H, XW], F32, kind="ExternalInput")
    wk_h = nc.dram_tensor("wkT", [H, H], F32, kind="ExternalInput")
    wv_h = nc.dram_tensor("wvT", [H, H], F32, kind="ExternalInput")
    wr_h = nc.dram_tensor("wrT", [H, H], F32, kind="ExternalInput")
    wo_h = nc.dram_tensor("woT", [H, H], F32, kind="ExternalInput")
    giu_h = nc.dram_tensor("giuT", [H, GUW], I32, kind="ExternalInput")
    did_h = nc.dram_tensor("didT", [INTER, H], I32, kind="ExternalInput")
    cols_h = nc.dram_tensor("colsin", [128, 81], F32, kind="ExternalInput")
    lut_h = nc.dram_tensor("lutrow", [1, 16], F32, kind="ExternalInput")
    y_h = nc.dram_tensor("y", [H, TC], F32, kind="ExternalOutput")

    ckvr = nc.dram_tensor("ckvr", [H, 3 * H], BF16, kind="Internal")
    cwo = nc.dram_tensor("cwo", [HT * H, 128], BF16, kind="Internal")
    cwgu = nc.dram_tensor("cwgu", [H, GUW], BF16, kind="Internal")
    cwd = nc.dram_tensor("cwd", [INTER, H], BF16, kind="Internal")

    from contextlib import ExitStack
    es = ExitStack()

    def sbt(name, shape, dt):
        return es.enter_context(nc.sbuf_tensor(name, shape, dt))

    xe = sbt("xe", [128, HT * TT], F32)
    xnb = sbt("xnb", [128, HT * 513], BF16)
    pool = sbt("pool", [128, 30208], BF16)
    zb = sbt("zb", [128, 8192], BF16)
    wbuf = sbt("wbuf", [128, 17408], BF16)
    idx32 = sbt("idx32", [128, 2 * CW], I32)
    trans = sbt("trans", [128, 3074], F32)
    colsb = sbt("colsb", [128, 81], F32)
    colsd = sbt("colsd", [128, 48], F32)
    lutb = sbt("lutb", [128, 16], F32)
    nstate = sbt("nstate", [128, 16], F32)
    dstate = sbt("dstate", [128, 16], F32)
    onesrow = sbt("onesrow", [1, 128], F32)
    onesb = sbt("onesb", [128, 1], BF16)
    constsb = sbt("constsb", [128, 2], F32)
    nc.const_aps.register(F32, 1e-5, constsb[:, 0:1])
    nc.const_aps.register(F32, -1.0, constsb[:, 1:2])

    pb = [es.enter_context(nc.psum_tensor("pb%d" % i, [128, 512], F32))
          for i in range(8)]

    def kin(h):
        return pool[:, h * TT:(h + 1) * TT]

    def vin(h):
        return pool[:, 8192 + h * TT: 8192 + (h + 1) * TT]

    def rin(h):
        return pool[:, 16384 + h * TT: 16384 + (h + 1) * TT]

    def ybt(h):
        return pool[:, h * TT:(h + 1) * TT]

    def hact(o):
        return pool[:, 8192 + o * TT: 8192 + (o + 1) * TT]

    def xet(h):
        return xe[:, h * TT:(h + 1) * TT]

    def xnt(h):
        return xnb[:, h * 513 + 1: h * 513 + 513]

    def xnprev(h):
        return xnb[:, h * 513: h * 513 + 512]

    def zt(a):
        return zb[:, a * TT:(a + 1) * TT]

    def x2n(h):
        return zb[:, h * TT:(h + 1) * TT]

    ekv = trans[:, 0:512]
    ta = trans[:, 512:1024]
    ewb = trans[:, 1024:1536]
    ns = trans[:, 1536:2049]
    ds = trans[:, 2049:2562]
    ekb = trans[:, 2562:3074]

    ewc = colsd[:, 0:16]
    etfc = colsd[:, 16:32]
    ttmp = colsd[:, 32:48]
    tdc = colsb[:, 0:16]
    tfc = colsb[:, 16:32]
    tmkc = colsb[:, 32:48]
    tmvc = colsb[:, 48:64]
    tmrc = colsb[:, 64:80]
    maskc = colsb[:, 80:81]

    mu = ds[0:1, 0:512]
    msq = ta[0:1, 0:512]
    tsm = ewb[0:1, 0:512]
    rstd = ns[0:1, 0:512]

    def kvrbuf(p):
        return bass.AP(wbuf, p * 6144, [[17408, 128], [384, 16], [1, 384]])

    def kvrtile(p, h, m):
        return wbuf[:, p * 6144 + h * 384 + m * 128: p * 6144 + h * 384 + (m + 1) * 128]

    def ybuf_ap(p):
        return bass.AP(wbuf, 12288 + p * 2048, [[17408, 128], [128, 16], [1, 128]])

    def ybtile(p, a):
        return wbuf[:, 12288 + p * 2048 + a * 128: 12288 + p * 2048 + (a + 1) * 128]

    def gubuf_ap(p):
        return bass.AP(wbuf, p * 4096, [[17408, 128], [256, 16], [1, 256]])

    def gutile(p, h, m):
        return wbuf[:, p * 4096 + h * 256 + m * 128: p * 4096 + h * 256 + (m + 1) * 128]

    def dbuf_ap(p):
        return bass.AP(wbuf, p * 5504, [[17408, 128], [128, 43], [1, 128]])

    def dtile(p, o):
        return wbuf[:, p * 5504 + o * 128: p * 5504 + (o + 1) * 128]

    sqb = [wbuf[:, 16384:16896], wbuf[:, 16896:17408]]

    idxb2 = [zb[:, 0:CW], zb[:, CW:2 * CW]]
    acc2 = zb[:, 2 * CW:3 * CW]
    accB = zb[:, 3 * CW:4 * CW]
    mrot = [pool[:, 0:CW], pool[:, CW:2 * CW], pool[:, 2 * CW:3 * CW]]
    stout = [pool[:, 3 * CW:4 * CW], pool[:, 4 * CW:5 * CW]]

    def xe_ap3():
        return bass.AP(xe, 0, [[HT * TT, 128], [TT, HT], [1, TT]])

    def x_hbm_ap(t):
        return bass.AP(x_h, TT * t, [[XW, 128], [128 * XW, HT], [1, TT]])

    def y_hbm_ap(t):
        return bass.AP(y_h, TT * t, [[TC, 128], [128 * TC, HT], [1, TT]])

    def script(ctx):
        W = ctx.wait
        O = ctx.op
        CK = ctx.ckpt

        O("Q", "dma_start", out=colsb[:, :], in_=cols_h[:, :], inc=16)
        O("Q", "dma_start", out=lutb[0:1, :], in_=lut_h[0:1, :], inc=16)
        CK("cols", "Q")
        O("V", "memset", onesrow[:, :], 1.0)
        O("V", "memset", onesb[:, :], 1.0)
        O("V", "memset", constsb[:, 0:1], 1e-5)
        O("V", "memset", constsb[:, 1:2], -1.0)
        O("V", "memset", nstate[:, :], 0.0)
        O("V", "memset", dstate[:, :], 0.0)
        O("V", "memset", bass.AP(xnb, 0, [[513 * HT, 128], [513, HT], [1, 1]]), 0.0,
          inc=1)
        CK("vinit", "V")
        W("T", "cols", "vinit")
        O("T", "matmul", out=pb[6][:, 0:16], lhsT=onesrow[:, :], rhs=lutb[0:1, :],
          start=True, stop=True, inc=1)
        CK("lutmm", "T")
        W("V", "lutmm")
        O("V", "tensor_copy", out=lutb[:, :], in_=pb[6][:, 0:16], inc=1)
        CK("lutb", "V")
        W("S", "cols", "vinit")
        O("S", "activation", out=ttmp, in_=tdc, func=AF.Exp, inc=1)
        CK("ttmpd", "S")
        W("S", "ttmpd")
        O("S", "activation", out=ewc, in_=ttmp, func=AF.Exp, scale=constsb[:, 1:2])
        O("S", "activation", out=etfc, in_=tfc, func=AF.Exp, inc=1)
        CK("chancols", "S")

        conv_jobs = []
        for m, wh in enumerate((wk_h, wv_h, wr_h)):
            for t in range(HT):
                conv_jobs.append(("kvr", m, t, wh))
        for t in range(HT):
            conv_jobs.append(("wo", 0, t, wo_h))
        for j, (kind, m, t, wh) in enumerate(conv_jobs):
            p = j % 2
            src32 = xe[:, p * 2048:(p + 1) * 2048]
            dst16 = wbuf[:, p * 2048:(p + 1) * 2048]
            if j >= 2:
                W("G", "convc%d" % (j - 2))
                W("G", "convs%d" % (j - 2))
            O("G", "dma_start", out=src32,
              in_=bass.AP(wh, t * 128 * H, [[H, 128], [1, 2048]]), inc=16)
            CK("convl%d" % j, "G")
            W("S", "convl%d" % j)
            O("S", "activation", out=dst16, in_=src32, func=AF.Copy, inc=1)
            CK("convc%d" % j, "S")
            W("G", "convc%d" % j)
            if kind == "kvr":
                dap = bass.AP(ckvr, (t * 128) * (3 * H) + m * 128,
                              [[3 * H, 128], [384, 16], [1, 128]])
            else:
                dap = bass.AP(cwo, (t * 128) * 128,
                              [[128, 128], [H * 128, 16], [1, 128]])
            O("G", "dma_start", out=dap,
              in_=bass.AP(wbuf, p * 2048, [[17408, 128], [128, 16], [1, 128]]),
              inc=16)
            CK("convs%d" % j, "G")
        ck_conv_done = "convs%d" % (len(conv_jobs) - 1)
        ck_conv_cast_done = "convc%d" % (len(conv_jobs) - 1)

        exp_jobs = []
        for t in range(HT):
            c0 = 0
            while c0 < GUW:
                cw = min(CW, GUW - c0)
                exp_jobs.append((giu_h, cwgu, t, c0, cw, GUW))
                c0 += cw
        for t in range(OT):
            exp_jobs.append((did_h, cwd, t, 0, H, H))
        W("V", "lutb")
        for j, (srch, dsth, rt, c0, cw, rowp) in enumerate(exp_jobs):
            p = j % 2
            if j >= 2:
                W("Q", "expc%d" % (j - 2))
            O("Q", "dma_start", out=idx32[:, p * CW:p * CW + cw],
              in_=bass.AP(srch, rt * 128 * rowp + c0, [[rowp, 128], [1, cw]]),
              inc=16)
            CK("expl%d" % j, "Q")
            W("S", "expl%d" % j)
            if j >= 2:
                W("S", "expu%d" % (j - 2))
            O("S", "activation", out=idxb2[p][:, 0:cw],
              in_=idx32[:, p * CW:p * CW + cw], func=AF.Copy, inc=1)
            CK("expc%d" % j, "S")
            W("V", "expc%d" % j)
            if j >= 2:
                W("V", "expd%d" % (j - 2))
            ib = idxb2[p][:, 0:cw]

            def _m(l, dst):
                O("V", "tensor_scalar", out=dst[:, 0:cw], in0=ib,
                  scalar1=float(l), scalar2=lutb[:, l:l + 1],
                  op0=OP.is_equal, op1=OP.mult)

            def _add(dst, src):
                O("V", "tensor_add", out=dst[:, 0:cw], in0=dst[:, 0:cw],
                  in1=src[:, 0:cw])

            _m(0, acc2)
            _m(1, accB)
            _m(2, mrot[0])
            _m(3, mrot[1])
            _m(4, mrot[2])
            accs = [acc2, accB]
            for l in range(5, 16):
                _add(accs[l % 2], mrot[(l - 5) % 3])
                _m(l, mrot[(l - 5) % 3])
            _add(accs[0], mrot[2])
            _add(accs[1], mrot[0])
            _add(accs[0], mrot[1])
            O("V", "tensor_add", out=stout[j % 2][:, 0:cw], in0=acc2[:, 0:cw],
              in1=accB[:, 0:cw], inc=1)
            CK("expu%d" % j, "V")
            W("Q", "expu%d" % j)
            O("Q", "dma_start",
              out=bass.AP(dsth, rt * 128 * rowp + c0, [[rowp, 128], [1, cw]]),
              in_=stout[j % 2][:, 0:cw], inc=16)
            CK("expd%d" % j, "Q")
        ck_exp_done = "expd%d" % (len(exp_jobs) - 1)
        ck_exp_udone = "expu%d" % (len(exp_jobs) - 1)

        W("Q", ck_conv_cast_done)
        O("Q", "dma_start", out=xe_ap3(), in_=x_hbm_ap(0), inc=16)
        CK("xl0", "Q")

        W("G", ck_conv_done)

        for i in range(NT):
            warm = (i == 0)

            if i == 0:
                wS = ["xl0", ck_exp_udone]
                wT = ["lutb"]
            elif i == 1:
                wS = ["xl1", "vmm0_15"]
                wT = []
            else:
                wS = ["xl%d" % i, "outv%d" % (i - 1)]
                wT = ["outv%d" % (i - 1)]
            for h in range(HT):
                if h == 0:
                    W("S", *wS)
                O("S", "activation", out=kin(h), in_=xet(h), func=AF.Copy)
                O("S", "activation", out=vin(h), in_=xet(h), func=AF.Square, inc=1)
                CK("acast%d_%d" % (i, h), "S")
                if h == 0:
                    W("T", *wT)
                W("T", "acast%d_%d" % (i, h))
                O("T", "matmul", out=pb[6][0:1, :], lhsT=onesb[:, :], rhs=kin(h),
                  start=(h == 0), stop=(h == HT - 1))
                O("T", "matmul", out=pb[7][0:1, :], lhsT=onesb[:, :], rhs=vin(h),
                  start=(h == 0), stop=(h == HT - 1),
                  inc=(1 if h == HT - 1 else 0))
            CK("astat%d" % i, "T")
            W("V", "astat%d" % i)
            O("V", "tensor_scalar", out=mu, in0=pb[6][0:1, :], scalar1=1.0 / H,
              scalar2=None, op0=OP.mult)
            O("V", "tensor_scalar", out=msq, in0=pb[7][0:1, :], scalar1=1.0 / H,
              scalar2=None, op0=OP.mult)
            O("V", "tensor_mul", out=tsm, in0=mu, in1=mu)
            O("V", "tensor_sub", out=msq, in0=msq, in1=tsm, inc=1)
            CK("avar%d" % i, "V")
            W("S", "avar%d" % i)
            O("S", "activation", out=rstd, in_=msq, func=AF.Sqrt, bias=constsb[0:1, 0:1], inc=1)
            CK("asqr%d" % i, "S")
            W("V", "asqr%d" % i)
            O("V", "reciprocal", out=rstd, in_=rstd, inc=1)
            CK("arstd%d" % i, "V")
            W("T", "arstd%d" % i)
            O("T", "matmul", out=pb[6][:, :], lhsT=onesrow[:, :], rhs=mu,
              start=True, stop=True)
            O("T", "matmul", out=pb[7][:, :], lhsT=onesrow[:, :], rhs=rstd,
              start=True, stop=True, inc=1)
            CK("abc%d" % i, "T")
            W("V", "abc%d" % i)
            for h in range(HT):
                O("V", "tensor_sub", out=ta, in0=xet(h), in1=pb[6][:, :])
                O("V", "tensor_mul", out=xnt(h), in0=ta, in1=pb[7][:, :],
                  inc=(1 if h == HT - 1 else 0))
            CK("axn%d" % i, "V")
            for h in range(HT):
                O("V", "tensor_sub", out=ta, in0=xnt(h), in1=xnprev(h))
                O("V", "scalar_tensor_tensor", out=kin(h), in0=ta,
                  scalar=tmkc[:, h:h + 1], in1=xnprev(h), op0=OP.mult, op1=OP.add)
                O("V", "scalar_tensor_tensor", out=vin(h), in0=ta,
                  scalar=tmvc[:, h:h + 1], in1=xnprev(h), op0=OP.mult, op1=OP.add)
                if not warm:
                    O("V", "scalar_tensor_tensor", out=rin(h), in0=ta,
                      scalar=tmrc[:, h:h + 1], in1=xnprev(h), op0=OP.mult,
                      op1=OP.add)
            O("V", "tensor_copy",
              out=bass.AP(xnb, 0, [[513 * HT, 128], [513, HT], [1, 1]]),
              in_=bass.AP(xnb, 512, [[513 * HT, 128], [513, HT], [1, 1]]), inc=1)
            CK("amix%d" % i, "V")

            for a in range(HT):
                pk = a % 2
                if a >= 2:
                    W("G", ("vmm%d_%d" if warm else "rmm%d_%d") % (i, a - 2))
                elif a == 0 and i == 1:
                    W("G", "vmm0_15")
                elif a == 0 and i >= 2:
                    W("G", "dmm%d_15" % (i - 1))
                O("G", "dma_start", out=kvrbuf(pk),
                  in_=bass.AP(ckvr, a * 384,
                              [[3 * H, 128], [128 * 3 * H, 16], [1, 384]]),
                  inc=16)
                CK("wkvr%d_%d" % (i, a), "G")
                if a == 0:
                    W("T", "amix%d" % i)
                if a >= 2:
                    W("T", ("scan%d_%d" if warm else "z%d_%d") % (i, a - 2))
                W("T", "wkvr%d_%d" % (i, a))
                for h in range(HT):
                    O("T", "matmul", out=pb[0 + pk][:, :], lhsT=kvrtile(pk, h, 0),
                      rhs=kin(h), start=(h == 0), stop=(h == HT - 1),
                      inc=(1 if h == HT - 1 else 0))
                CK("kmm%d_%d" % (i, a), "T")
                for h in range(HT):
                    O("T", "matmul", out=pb[2 + pk][:, :], lhsT=kvrtile(pk, h, 1),
                      rhs=vin(h), start=(h == 0), stop=(h == HT - 1),
                      inc=(1 if h == HT - 1 else 0))
                CK("vmm%d_%d" % (i, a), "T")
                if not warm:
                    for h in range(HT):
                        O("T", "matmul", out=pb[4 + pk][:, :],
                          lhsT=kvrtile(pk, h, 2), rhs=rin(h),
                          start=(h == 0), stop=(h == HT - 1),
                          inc=(1 if h == HT - 1 else 0))
                    CK("rmm%d_%d" % (i, a), "T")
                W("S", "kmm%d_%d" % (i, a))
                if a >= 1:
                    W("S", "scan%d_%d" % (i, a - 1))
                O("S", "activation", out=ekb, in_=pb[0 + pk][:, :],
                  func=AF.Exp, inc=1)
                CK("ekx%d_%d" % (i, a), "S")
                if warm:
                    W("S", "ekx%d_%d" % (i, a))
                    O("S", "mul", out=ekb, in_=ekb, mul=maskc, inc=1)
                CK("ek%d_%d" % (i, a), "S")
                if not warm:
                    W("S", "rmm%d_%d" % (i, a))
                    O("S", "activation", out=pb[4 + pk][:, :], in_=pb[4 + pk][:, :],
                      func=AF.Sigmoid, inc=1)
                    CK("sig%d_%d" % (i, a), "S")
                O("V", "memset", ewb, 1.0)
                O("V", "tensor_scalar", out=ewb, in0=ewb,
                  scalar1=ewc[:, a:a + 1], scalar2=None, op0=OP.mult)
                W("V", "ek%d_%d" % (i, a), "vmm%d_%d" % (i, a))
                O("V", "tensor_mul", out=ekv, in0=ekb,
                  in1=pb[2 + pk][:, :])
                O("V", "tensor_copy", out=ns[:, 0:1], in_=nstate[:, a:a + 1])
                O("V", "tensor_copy", out=ds[:, 0:1], in_=dstate[:, a:a + 1])
                O("V", "tensor_tensor_scan", out=ns[:, 1:513], data0=ewb,
                  data1=ekv, initial=nstate[:, a:a + 1], op0=OP.mult, op1=OP.add)
                O("V", "tensor_tensor_scan", out=ds[:, 1:513], data0=ewb,
                  data1=ekb, initial=dstate[:, a:a + 1],
                  op0=OP.mult, op1=OP.add)
                O("V", "tensor_copy", out=nstate[:, a:a + 1], in_=ns[:, 512:513])
                O("V", "tensor_copy", out=dstate[:, a:a + 1], in_=ds[:, 512:513],
                  inc=1)
                CK("scan%d_%d" % (i, a), "V")
                if warm:
                    continue
                W("S", "scan%d_%d" % (i, a))
                O("S", "activation", out=ekv, in_=ekb, func=AF.Copy,
                  scale=etfc[:, a:a + 1], inc=1)
                CK("ektf%d_%d" % (i, a), "S")
                W("V", "ektf%d_%d" % (i, a))
                O("V", "tensor_mul", out=ta, in0=ekv, in1=pb[2 + pk][:, :])
                O("V", "tensor_add", out=ta, in0=ta, in1=ns[:, 0:512])
                O("V", "tensor_add", out=ns[:, 0:512], in0=ekv, in1=ds[:, 0:512])
                O("V", "reciprocal", out=ns[:, 0:512], in_=ns[:, 0:512])
                W("V", "sig%d_%d" % (i, a))
                O("V", "tensor_mul", out=ta, in0=ta, in1=ns[:, 0:512])
                O("V", "tensor_mul", out=zt(a), in0=ta, in1=pb[4 + pk][:, :],
                  inc=1)
                CK("z%d_%d" % (i, a), "V")

            if warm:
                W("Q", "axn%d" % i)
                O("Q", "dma_start", out=xe_ap3(), in_=x_hbm_ap(i + 1), inc=16)
                CK("xl%d" % (i + 1), "Q")
                continue

            for h in range(HT):
                ph = h % 2
                if h >= 2:
                    W("G", "ymm%d_%d" % (i, h - 2))
                O("G", "dma_start", out=ybuf_ap(ph),
                  in_=bass.AP(cwo, h * H * 128,
                              [[128, 128], [128 * 128, 16], [1, 128]]), inc=16)
                CK("wy%d_%d" % (i, h), "G")
                if h == 0:
                    W("T", "z%d_15" % i)
                if h >= 2:
                    W("T", "ybc%d_%d" % (i, h - 2))
                W("T", "wy%d_%d" % (i, h))
                for a in range(HT):
                    O("T", "matmul", out=pb[0 + ph][:, :], lhsT=ybtile(ph, a),
                      rhs=zt(a), start=(a == 0), stop=(a == HT - 1),
                      inc=(1 if a == HT - 1 else 0))
                CK("ymm%d_%d" % (i, h), "T")
                W("S", "ymm%d_%d" % (i, h))
                O("S", "activation", out=ybt(h), in_=pb[0 + ph][:, :], func=AF.Copy,
                  inc=1)
                CK("ybc%d_%d" % (i, h), "S")

            for h in range(HT):
                W("S", "ybc%d_%d" % (i, h))
                if h >= 2:
                    W("S", "bsm%d_%d" % (i, h - 2))
                O("S", "activation", out=sqb[h % 2], in_=ybt(h), func=AF.Square,
                  inc=1)
                CK("bsq%d_%d" % (i, h), "S")
                W("T", "bsq%d_%d" % (i, h))
                O("T", "matmul", out=pb[6][0:1, :], lhsT=onesb[:, :], rhs=ybt(h),
                  start=(h == 0), stop=(h == HT - 1))
                O("T", "matmul", out=pb[7][0:1, :], lhsT=onesb[:, :],
                  rhs=sqb[h % 2], start=(h == 0), stop=(h == HT - 1), inc=1)
                CK("bsm%d_%d" % (i, h), "T")
            CK("bstat%d" % i, "T")
            W("V", "bstat%d" % i)
            O("V", "tensor_scalar", out=mu, in0=pb[6][0:1, :], scalar1=1.0 / H,
              scalar2=None, op0=OP.mult)
            O("V", "tensor_scalar", out=msq, in0=pb[7][0:1, :], scalar1=1.0 / H,
              scalar2=None, op0=OP.mult)
            O("V", "tensor_mul", out=tsm, in0=mu, in1=mu)
            O("V", "tensor_sub", out=msq, in0=msq, in1=tsm, inc=1)
            CK("bvar%d" % i, "V")
            W("S", "bvar%d" % i)
            O("S", "activation", out=rstd, in_=msq, func=AF.Sqrt, bias=constsb[0:1, 0:1], inc=1)
            CK("bsqr%d" % i, "S")
            W("V", "bsqr%d" % i)
            O("V", "reciprocal", out=rstd, in_=rstd, inc=1)
            CK("brstd%d" % i, "V")
            W("T", "brstd%d" % i)
            O("T", "matmul", out=pb[6][:, :], lhsT=onesrow[:, :], rhs=mu,
              start=True, stop=True)
            O("T", "matmul", out=pb[7][:, :], lhsT=onesrow[:, :], rhs=rstd,
              start=True, stop=True, inc=1)
            CK("bbc%d" % i, "T")
            W("V", "bbc%d" % i)
            for h in range(HT):
                O("V", "tensor_sub", out=ta, in0=ybt(h), in1=pb[6][:, :])
                O("V", "tensor_mul", out=ta, in0=ta, in1=pb[7][:, :])
                O("V", "tensor_add", out=xet(h), in0=xet(h), in1=ta,
                  inc=(1 if h == HT - 1 else 0))
            CK("x1%d" % i, "V")

            for h in range(HT):
                if h == 0:
                    W("S", "x1%d" % i)
                if h >= 2:
                    W("S", "csm%d_%d" % (i, h - 2))
                O("S", "activation", out=rin(h), in_=xet(h), func=AF.Copy)
                O("S", "activation", out=sqb[h % 2], in_=xet(h), func=AF.Square,
                  inc=1)
                CK("csq%d_%d" % (i, h), "S")
                if h == 0:
                    W("T", "x1%d" % i)
                W("T", "csq%d_%d" % (i, h))
                O("T", "matmul", out=pb[6][0:1, :], lhsT=onesb[:, :], rhs=rin(h),
                  start=(h == 0), stop=(h == HT - 1))
                O("T", "matmul", out=pb[7][0:1, :], lhsT=onesb[:, :],
                  rhs=sqb[h % 2], start=(h == 0), stop=(h == HT - 1), inc=1)
                CK("csm%d_%d" % (i, h), "T")
            CK("cstat%d" % i, "T")
            W("V", "cstat%d" % i)
            O("V", "tensor_scalar", out=mu, in0=pb[6][0:1, :], scalar1=1.0 / H,
              scalar2=None, op0=OP.mult)
            O("V", "tensor_scalar", out=msq, in0=pb[7][0:1, :], scalar1=1.0 / H,
              scalar2=None, op0=OP.mult)
            O("V", "tensor_mul", out=tsm, in0=mu, in1=mu)
            O("V", "tensor_sub", out=msq, in0=msq, in1=tsm, inc=1)
            CK("cvar%d" % i, "V")
            W("S", "cvar%d" % i)
            O("S", "activation", out=rstd, in_=msq, func=AF.Sqrt, bias=constsb[0:1, 0:1], inc=1)
            CK("csqr%d" % i, "S")
            W("V", "csqr%d" % i)
            O("V", "reciprocal", out=rstd, in_=rstd, inc=1)
            CK("crstd%d" % i, "V")
            W("T", "crstd%d" % i)
            O("T", "matmul", out=pb[6][:, :], lhsT=onesrow[:, :], rhs=mu,
              start=True, stop=True)
            O("T", "matmul", out=pb[7][:, :], lhsT=onesrow[:, :], rhs=rstd,
              start=True, stop=True, inc=1)
            CK("cbc%d" % i, "T")
            W("V", "cbc%d" % i)
            for h in range(HT):
                O("V", "tensor_sub", out=ta, in0=xet(h), in1=pb[6][:, :])
                O("V", "tensor_mul", out=x2n(h), in0=ta, in1=pb[7][:, :],
                  inc=(1 if h == HT - 1 else 0))
            CK("x2n%d" % i, "V")

            for o in range(OT):
                po = o % 2
                if o < 2:
                    W("Q", ck_exp_done, "rmm%d_15" % i)
                else:
                    W("Q", "gumm%d_%d" % (i, o - 2))
                O("Q", "dma_start", out=gubuf_ap(po),
                  in_=bass.AP(cwgu, o * 256,
                              [[GUW, 128], [128 * GUW, 16], [1, 256]]), inc=16)
                CK("wgu%d_%d" % (i, o), "Q")
                if o == 0:
                    W("T", "x2n%d" % i)
                if o >= 2:
                    W("T", "hact%d_%d" % (i, o - 2))
                W("T", "wgu%d_%d" % (i, o))
                for h in range(HT):
                    O("T", "matmul", out=pb[0 + po][:, :], lhsT=gutile(po, h, 0),
                      rhs=x2n(h), start=(h == 0), stop=(h == HT - 1),
                      inc=(1 if h == HT - 1 else 0))
                CK("gmm%d_%d" % (i, o), "T")
                for h in range(HT):
                    O("T", "matmul", out=pb[2 + po][:, :], lhsT=gutile(po, h, 1),
                      rhs=x2n(h), start=(h == 0), stop=(h == HT - 1),
                      inc=(1 if h == HT - 1 else 0))
                CK("gumm%d_%d" % (i, o), "T")
                W("S", "gmm%d_%d" % (i, o))
                O("S", "activation", out=sqb[o % 2], in_=pb[0 + po][:, :],
                  func=AF.Silu, inc=1)
                CK("silu%d_%d" % (i, o), "S")
                W("V", "silu%d_%d" % (i, o), "gumm%d_%d" % (i, o))
                O("V", "tensor_mul", out=hact(o), in0=sqb[o % 2],
                  in1=pb[2 + po][:, :], inc=1)
                CK("hact%d_%d" % (i, o), "V")

            for h in range(HT):
                ph = h % 2
                if h == 0:
                    W("G", "gumm%d_%d" % (i, OT - 1))
                if h >= 2:
                    W("G", "dmm%d_%d" % (i, h - 2))
                O("G", "dma_start", out=dbuf_ap(ph),
                  in_=bass.AP(cwd, h * 128, [[H, 128], [128 * H, 43], [1, 128]]),
                  inc=16)
                CK("wd%d_%d" % (i, h), "G")
                if h == 0:
                    W("T", "hact%d_42" % i)
                if h >= 2:
                    W("T", "y2c%d_%d" % (i, h - 2))
                W("T", "wd%d_%d" % (i, h))
                for o in range(OT):
                    O("T", "matmul", out=pb[4 + ph][:, :], lhsT=dtile(ph, o),
                      rhs=hact(o), start=(o == 0), stop=(o == OT - 1),
                      inc=(1 if o == OT - 1 else 0))
                CK("dmm%d_%d" % (i, h), "T")
                W("S", "dmm%d_%d" % (i, h))
                O("S", "activation", out=ybt(h), in_=pb[4 + ph][:, :],
                  func=AF.Copy, inc=1)
                CK("y2c%d_%d" % (i, h), "S")

            for h in range(HT):
                W("S", "y2c%d_%d" % (i, h))
                if h >= 2:
                    W("S", "dsm%d_%d" % (i, h - 2))
                O("S", "activation", out=sqb[h % 2], in_=ybt(h), func=AF.Square,
                  inc=1)
                CK("dsq%d_%d" % (i, h), "S")
                if h == 0:
                    W("T", "x2n%d" % i)
                W("T", "dsq%d_%d" % (i, h))
                O("T", "matmul", out=pb[6][0:1, :], lhsT=onesb[:, :], rhs=ybt(h),
                  start=(h == 0), stop=(h == HT - 1))
                O("T", "matmul", out=pb[7][0:1, :], lhsT=onesb[:, :],
                  rhs=sqb[h % 2], start=(h == 0), stop=(h == HT - 1), inc=1)
                CK("dsm%d_%d" % (i, h), "T")
            CK("dstat%d" % i, "T")
            W("V", "dstat%d" % i)
            O("V", "tensor_scalar", out=mu, in0=pb[6][0:1, :], scalar1=1.0 / H,
              scalar2=None, op0=OP.mult)
            O("V", "tensor_scalar", out=msq, in0=pb[7][0:1, :], scalar1=1.0 / H,
              scalar2=None, op0=OP.mult)
            O("V", "tensor_mul", out=tsm, in0=mu, in1=mu)
            O("V", "tensor_sub", out=msq, in0=msq, in1=tsm, inc=1)
            CK("dvar%d" % i, "V")
            W("S", "dvar%d" % i)
            O("S", "activation", out=rstd, in_=msq, func=AF.Sqrt, bias=constsb[0:1, 0:1], inc=1)
            CK("dsqr%d" % i, "S")
            W("V", "dsqr%d" % i)
            O("V", "reciprocal", out=rstd, in_=rstd, inc=1)
            CK("drstd%d" % i, "V")
            W("T", "drstd%d" % i)
            O("T", "matmul", out=pb[6][:, :], lhsT=onesrow[:, :], rhs=mu,
              start=True, stop=True)
            O("T", "matmul", out=pb[7][:, :], lhsT=onesrow[:, :], rhs=rstd,
              start=True, stop=True, inc=1)
            CK("dbc%d" % i, "T")
            W("V", "dbc%d" % i)
            for h in range(HT):
                O("V", "tensor_sub", out=ta, in0=ybt(h), in1=pb[6][:, :])
                O("V", "tensor_mul", out=ta, in0=ta, in1=pb[7][:, :])
                O("V", "tensor_add", out=xet(h), in0=xet(h), in1=ta,
                  inc=(1 if h == HT - 1 else 0))
            CK("outv%d" % i, "V")

            W("Q", "outv%d" % i)
            O("Q", "dma_start", out=y_hbm_ap(i - 1), in_=xe_ap3(), inc=16)
            CK("outd%d" % i, "Q")
            if i + 1 < NT:
                W("Q", "outd%d" % i)
                O("Q", "dma_start", out=xe_ap3(), in_=x_hbm_ap(i + 1), inc=16)
                CK("xl%d" % (i + 1), "Q")

        if ctx.mode == "dry":
            ctx.ck["END_Q"] = [("Q" + str(r), ctx.cnt["Q" + str(r)])
                               for r in range(NRING)
                               if ctx.cnt["Q" + str(r)] > 0]
        if ctx.mode == "Q":
            for src, val in ctx.ck["END_Q"]:
                ctx.handles["Q"].wait_ge(ctx.sems[src], val)

    dry = Ctx("dry")
    script(dry)

    with es:
        with ExitStack() as ses:
            sems = {}
            for k in "TVS":
                sems[k] = ses.enter_context(nc.semaphore("s" + k))
            for q in "QG":
                for r in range(NRING):
                    sems[q + str(r)] = ses.enter_context(
                        nc.semaphore("s%s%d" % (q, r)))
            block = ses.enter_context(nc.Block())

            @block.tensor
            def _(eng):
                script(Ctx("T", {"T": eng}, sems, dry.ck))

            @block.vector
            def _(eng):
                script(Ctx("V", {"V": eng}, sems, dry.ck))

            @block.scalar
            def _(eng):
                script(Ctx("S", {"S": eng}, sems, dry.ck))

            @block.sync
            def _(eng):
                script(Ctx("Q", {"Q": eng}, sems, dry.ck))

            @block.gpsimd
            def _(eng):
                script(Ctx("G", {"G": eng}, sems, dry.ck))

    return nc


def kernel(L0_attn_hidden, L0_attn_time_decay, L0_attn_time_first,
           L0_attn_time_mix_key, L0_attn_time_mix_value,
           L0_attn_time_mix_receptance, L0_attn_W_key, L0_attn_W_value,
           L0_attn_W_receptance, L0_attn_W_output, L0_ffn_lut,
           L0_ffn_gate_idx, L0_ffn_up_idx, L0_ffn_down_idx):
    B, T, _ = L0_attn_hidden.shape

    wkT = np.ascontiguousarray(np.asarray(L0_attn_W_key, np.float32).T)
    wvT = np.ascontiguousarray(np.asarray(L0_attn_W_value, np.float32).T)
    wrT = np.ascontiguousarray(np.asarray(L0_attn_W_receptance, np.float32).T)
    woT = np.ascontiguousarray(np.asarray(L0_attn_W_output, np.float32).T)

    g = np.asarray(L0_ffn_gate_idx, np.int32).T.reshape(H, OT, 128)
    u = np.asarray(L0_ffn_up_idx, np.int32).T.reshape(H, OT, 128)
    giuT = np.ascontiguousarray(np.stack([g, u], axis=2).reshape(H, GUW))
    didT = np.ascontiguousarray(np.asarray(L0_ffn_down_idx, np.int32).T)

    lutrow = np.ascontiguousarray(
        np.asarray(L0_ffn_lut, np.float32).reshape(1, 16))

    def colsify(v):
        return np.asarray(v, np.float32).reshape(HT, 128).T

    x_full = np.asarray(L0_attn_hidden, np.float32)

    in_maps = []
    for c in range(8):
        b, q = c // 4, c % 4
        t0 = q * TC
        xs = np.zeros((XW, H), np.float32)
        if q == 0:
            xs[512:] = x_full[b, 0:TC]
        else:
            xs[:] = x_full[b, t0 - 512:t0 + TC]
        x_fm = np.ascontiguousarray(xs.T)

        cols = np.zeros((128, 81), np.float32)
        cols[:, 0:16] = colsify(L0_attn_time_decay)
        cols[:, 16:32] = colsify(L0_attn_time_first)
        cols[:, 32:48] = colsify(np.asarray(L0_attn_time_mix_key).reshape(-1))
        cols[:, 48:64] = colsify(np.asarray(L0_attn_time_mix_value).reshape(-1))
        cols[:, 64:80] = colsify(
            np.asarray(L0_attn_time_mix_receptance).reshape(-1))
        cols[:, 80] = 0.0 if q == 0 else 1.0

        in_maps.append({
            "x": x_fm, "wkT": wkT, "wvT": wvT, "wrT": wrT, "woT": woT,
            "giuT": giuT, "didT": didT, "colsin": cols, "lutrow": lutrow,
        })

    nc = build_program()
    res = run_bass_kernel_spmd(nc, in_maps, core_ids=list(range(8)))

    out = np.empty((B, T, H), np.float32)
    for c in range(8):
        b, q = c // 4, c % 4
        out[b, q * TC:(q + 1) * TC, :] = res.results[c]["y"].T
    return out


# version 13
# speedup vs baseline: 1.0711x; 1.0024x over previous
import numpy as np
import concourse.bass as bass
import concourse.mybir as mybir
from concourse.bass_utils import run_bass_kernel_spmd

F32 = mybir.dt.float32
BF16 = mybir.dt.bfloat16
I32 = mybir.dt.int32
AF = mybir.ActivationFunctionType
OP = mybir.AluOpType
PM = mybir.MatmulPerfMode.DoubleRow

H = 2048
HT = 16
TT = 512
NT = 5
TC = 2048
XW = 512 + TC
INTER = 5504
OT = 43
GUW = 2 * INTER
CW = 2048


NRING = 8
HAZ_WIN = 4

_IN_KEYS = ("in_", "in0", "in1", "data0", "data1", "initial", "scalar",
            "scalar1", "scalar2", "scale", "bias", "mul", "mask",
            "on_true", "on_false", "lhsT", "rhs", "weights")


def _ap_region(v):
    try:
        t = v.tensor
        name = t.name
        off = int(v.offset)
        span = 1
        for pair in v.ap:
            st, n = int(pair[0]), int(pair[1])
            span += st * (n - 1)
        return (name, off, off + span)
    except Exception:
        return None


class Ctx:

    def __init__(self, mode, handles=None, sems=None, ck=None):
        self.mode = mode
        self.handles = handles or {}
        self.sems = sems or {}
        self.ck = ck if ck is not None else {}
        self.cnt = {}
        for k in "TVS":
            self.cnt[k] = 0
        for q in "QG":
            for r in range(NRING):
                self.cnt[q + str(r)] = 0
        self.dmaj = {"Q": 0, "G": 0}
        self.lastw = {k: [] for k in "TVS"}
        self.hiwater = {}

    def op(self, eng, name, *args, inc=0, **kw):
        if name == "dma_start":
            j = self.dmaj[eng]
            self.dmaj[eng] = j + 1
            key = eng + str(j % NRING)
            if self.mode == "dry":
                self.cnt[key] = 16 * (j // NRING + 1)
                return
            if self.mode != eng:
                return
            if j >= NRING and 16 * (j // NRING) > self.hiwater.get(key, 0):
                self.hiwater[key] = 16 * (j // NRING)
                self.handles[eng].wait_ge(self.sems[key], 16 * (j // NRING))
            ins = getattr(self.handles[eng], name)(*args, **kw)
            ins.then_inc(self.sems[key], 16)
            return
        ins_regions = []
        out_regions = []
        if name == "memset":
            r = _ap_region(args[0])
            if r:
                out_regions.append(r)
        else:
            for k, v in kw.items():
                r = _ap_region(v)
                if r is None:
                    continue
                if k in ("out", "accum_out"):
                    out_regions.append(r)
                elif k in _IN_KEYS:
                    ins_regions.append(r)
        def _ovl(a, b):
            for (an, aa, ab) in a:
                for (bn, ba, bb) in b:
                    if an == bn and aa < bb and ba < ab:
                        return True
            return False

        need = 0
        for wval, wregs, rregs in self.lastw[eng]:
            if (_ovl(wregs, ins_regions) or _ovl(rregs, out_regions)
                    or _ovl(wregs, out_regions)):
                need = max(need, wval)
        myval = self.cnt[eng] + 1
        self.cnt[eng] = myval
        self.lastw[eng].append((myval, out_regions, ins_regions))
        if len(self.lastw[eng]) > HAZ_WIN:
            self.lastw[eng].pop(0)
        if self.mode == "dry":
            return
        if self.mode != eng:
            return
        if need and need > self.hiwater.get(eng, 0):
            self.hiwater[eng] = need
            self.handles[eng].wait_ge(self.sems[eng], need)
        ins = getattr(self.handles[eng], name)(*args, **kw)
        ins.then_inc(self.sems[eng], 1)

    def ckpt(self, name, eng):
        if self.mode == "dry":
            assert name not in self.ck, name
            if eng in ("Q", "G"):
                self.ck[name] = [(eng + str(r), self.cnt[eng + str(r)])
                                 for r in range(NRING)
                                 if self.cnt[eng + str(r)] > 0]
            else:
                self.ck[name] = [(eng, self.cnt[eng])]

    def wait(self, eng, *names):
        if self.mode == "dry":
            for n in names:
                assert n in self.ck, n
            return
        if self.mode != eng:
            return
        for n in names:
            for src, val in self.ck[n]:
                if val > 0 and val > self.hiwater.get(src, 0):
                    self.hiwater[src] = val
                    self.handles[eng].wait_ge(self.sems[src], val)


def build_program():
    nc = bass.Bass("TRN2", target_bir_lowering=False)

    x_h = nc.dram_tensor("x", [H, XW], F32, kind="ExternalInput")
    wk_h = nc.dram_tensor("wkT", [H, H], F32, kind="ExternalInput")
    wv_h = nc.dram_tensor("wvT", [H, H], F32, kind="ExternalInput")
    wr_h = nc.dram_tensor("wrT", [H, H], F32, kind="ExternalInput")
    wo_h = nc.dram_tensor("woT", [H, H], F32, kind="ExternalInput")
    giu_h = nc.dram_tensor("giuT", [H, GUW], I32, kind="ExternalInput")
    did_h = nc.dram_tensor("didT", [INTER, H], I32, kind="ExternalInput")
    cols_h = nc.dram_tensor("colsin", [128, 81], F32, kind="ExternalInput")
    lut_h = nc.dram_tensor("lutrow", [1, 16], F32, kind="ExternalInput")
    y_h = nc.dram_tensor("y", [H, TC], F32, kind="ExternalOutput")

    ckvr = nc.dram_tensor("ckvr", [H, 3 * H], BF16, kind="Internal")
    cwo = nc.dram_tensor("cwo", [HT * H, 128], BF16, kind="Internal")
    cwgu = nc.dram_tensor("cwgu", [H, GUW], BF16, kind="Internal")
    cwd = nc.dram_tensor("cwd", [INTER, H], BF16, kind="Internal")

    from contextlib import ExitStack
    es = ExitStack()

    def sbt(name, shape, dt):
        return es.enter_context(nc.sbuf_tensor(name, shape, dt))

    xe = sbt("xe", [128, HT * TT], F32)
    xnb = sbt("xnb", [128, HT * 513], BF16)
    pool = sbt("pool", [128, 30208], BF16)
    zb = sbt("zb", [128, 8192], BF16)
    wbuf = sbt("wbuf", [128, 17408], BF16)
    idx32 = sbt("idx32", [128, 2 * CW], I32)
    trans = sbt("trans", [128, 3074], F32)
    colsb = sbt("colsb", [128, 81], F32)
    colsd = sbt("colsd", [128, 48], F32)
    lutb = sbt("lutb", [128, 16], F32)
    nstate = sbt("nstate", [128, 16], F32)
    dstate = sbt("dstate", [128, 16], F32)
    onesrow = sbt("onesrow", [1, 128], F32)
    onesb = sbt("onesb", [128, 1], BF16)
    constsb = sbt("constsb", [128, 2], F32)
    nc.const_aps.register(F32, 1e-5, constsb[:, 0:1])
    nc.const_aps.register(F32, -1.0, constsb[:, 1:2])

    pb = [es.enter_context(nc.psum_tensor("pb%d" % i, [128, 512], F32))
          for i in range(8)]

    def kin(h):
        return pool[:, h * TT:(h + 1) * TT]

    def vin(h):
        return pool[:, 8192 + h * TT: 8192 + (h + 1) * TT]

    def rin(h):
        return pool[:, 16384 + h * TT: 16384 + (h + 1) * TT]

    def ybt(h):
        return pool[:, h * TT:(h + 1) * TT]

    def hact(o):
        return pool[:, 8192 + o * TT: 8192 + (o + 1) * TT]

    def xet(h):
        return xe[:, h * TT:(h + 1) * TT]

    def xnt(h):
        return xnb[:, h * 513 + 1: h * 513 + 513]

    def xnprev(h):
        return xnb[:, h * 513: h * 513 + 512]

    def zt(a):
        return zb[:, a * TT:(a + 1) * TT]

    def x2n(h):
        return zb[:, h * TT:(h + 1) * TT]

    ekv = trans[:, 0:512]
    ta = trans[:, 512:1024]
    ewb = trans[:, 1024:1536]
    ns = trans[:, 1536:2049]
    ds = trans[:, 2049:2562]
    ekb = trans[:, 2562:3074]

    ewc = colsd[:, 0:16]
    etfc = colsd[:, 16:32]
    ttmp = colsd[:, 32:48]
    tdc = colsb[:, 0:16]
    tfc = colsb[:, 16:32]
    tmkc = colsb[:, 32:48]
    tmvc = colsb[:, 48:64]
    tmrc = colsb[:, 64:80]
    maskc = colsb[:, 80:81]

    mu = ds[0:1, 0:512]
    msq = ta[0:1, 0:512]
    tsm = ewb[0:1, 0:512]
    rstd = ns[0:1, 0:512]

    def kvrbuf(p):
        return bass.AP(wbuf, p * 6144, [[17408, 128], [384, 16], [1, 384]])

    def kvrtile(p, h, m):
        return wbuf[:, p * 6144 + h * 384 + m * 128: p * 6144 + h * 384 + (m + 1) * 128]

    def ybuf_ap(p):
        return bass.AP(wbuf, 12288 + p * 2048, [[17408, 128], [128, 16], [1, 128]])

    def ybtile(p, a):
        return wbuf[:, 12288 + p * 2048 + a * 128: 12288 + p * 2048 + (a + 1) * 128]

    def gubuf_ap(p):
        return bass.AP(wbuf, p * 4096, [[17408, 128], [256, 16], [1, 256]])

    def gutile(p, h, m):
        return wbuf[:, p * 4096 + h * 256 + m * 128: p * 4096 + h * 256 + (m + 1) * 128]

    def dbuf_ap(p):
        return bass.AP(wbuf, p * 5504, [[17408, 128], [128, 43], [1, 128]])

    def dtile(p, o):
        return wbuf[:, p * 5504 + o * 128: p * 5504 + (o + 1) * 128]

    sqb = [wbuf[:, 16384:16896], wbuf[:, 16896:17408]]

    idxb2 = [zb[:, 0:CW], zb[:, CW:2 * CW]]
    acc2 = zb[:, 2 * CW:3 * CW]
    accB = zb[:, 3 * CW:4 * CW]
    mrot = [pool[:, 0:CW], pool[:, CW:2 * CW], pool[:, 2 * CW:3 * CW]]

    def xe_ap3():
        return bass.AP(xe, 0, [[HT * TT, 128], [TT, HT], [1, TT]])

    def x_hbm_ap(t):
        return bass.AP(x_h, TT * t, [[XW, 128], [128 * XW, HT], [1, TT]])

    def y_hbm_ap(t):
        return bass.AP(y_h, TT * t, [[TC, 128], [128 * TC, HT], [1, TT]])

    def script(ctx):
        W = ctx.wait
        O = ctx.op
        CK = ctx.ckpt

        O("Q", "dma_start", out=colsb[:, :], in_=cols_h[:, :], inc=16)
        O("Q", "dma_start", out=lutb[0:1, :], in_=lut_h[0:1, :], inc=16)
        CK("cols", "Q")
        O("V", "memset", onesrow[:, :], 1.0)
        O("V", "memset", onesb[:, :], 1.0)
        O("V", "memset", constsb[:, 0:1], 1e-5)
        O("V", "memset", constsb[:, 1:2], -1.0)
        O("V", "memset", nstate[:, :], 0.0)
        O("V", "memset", dstate[:, :], 0.0)
        O("V", "memset", bass.AP(xnb, 0, [[513 * HT, 128], [513, HT], [1, 1]]), 0.0,
          inc=1)
        CK("vinit", "V")
        W("T", "cols", "vinit")
        O("T", "matmul", out=pb[6][:, 0:16], lhsT=onesrow[:, :], rhs=lutb[0:1, :],
          start=True, stop=True, inc=1)
        CK("lutmm", "T")
        W("V", "lutmm")
        O("V", "tensor_copy", out=lutb[:, :], in_=pb[6][:, 0:16], inc=1)
        CK("lutb", "V")
        W("S", "cols", "vinit")
        O("S", "activation", out=ttmp, in_=tdc, func=AF.Exp, inc=1)
        CK("ttmpd", "S")
        W("S", "ttmpd")
        O("S", "activation", out=ewc, in_=ttmp, func=AF.Exp, scale=constsb[:, 1:2])
        O("S", "activation", out=etfc, in_=tfc, func=AF.Exp, inc=1)
        CK("chancols", "S")

        conv_jobs = []
        for m, wh in enumerate((wk_h, wv_h, wr_h)):
            for t in range(HT):
                conv_jobs.append(("kvr", m, t, wh))
        for t in range(HT):
            conv_jobs.append(("wo", 0, t, wo_h))
        for j, (kind, m, t, wh) in enumerate(conv_jobs):
            p = j % 2
            src32 = xe[:, p * 2048:(p + 1) * 2048]
            dst16 = wbuf[:, p * 2048:(p + 1) * 2048]
            if j >= 2:
                W("G", "convc%d" % (j - 2))
                W("G", "convs%d" % (j - 2))
            O("G", "dma_start", out=src32,
              in_=bass.AP(wh, t * 128 * H, [[H, 128], [1, 2048]]), inc=16)
            CK("convl%d" % j, "G")
            W("S", "convl%d" % j)
            O("S", "activation", out=dst16, in_=src32, func=AF.Copy, inc=1)
            CK("convc%d" % j, "S")
            W("G", "convc%d" % j)
            if kind == "kvr":
                dap = bass.AP(ckvr, (t * 128) * (3 * H) + m * 128,
                              [[3 * H, 128], [384, 16], [1, 128]])
            else:
                dap = bass.AP(cwo, (t * 128) * 128,
                              [[128, 128], [H * 128, 16], [1, 128]])
            O("G", "dma_start", out=dap,
              in_=bass.AP(wbuf, p * 2048, [[17408, 128], [128, 16], [1, 128]]),
              inc=16)
            CK("convs%d" % j, "G")
        ck_conv_done = "convs%d" % (len(conv_jobs) - 1)
        ck_conv_cast_done = "convc%d" % (len(conv_jobs) - 1)

        exp_jobs = []
        for t in range(HT):
            c0 = 0
            while c0 < GUW:
                cw = min(CW, GUW - c0)
                exp_jobs.append((giu_h, cwgu, t, c0, cw, GUW))
                c0 += cw
        for t in range(OT):
            exp_jobs.append((did_h, cwd, t, 0, H, H))
        W("V", "lutb")
        for j, (srch, dsth, rt, c0, cw, rowp) in enumerate(exp_jobs):
            p = j % 2
            if j >= 2:
                W("Q", "expc%d" % (j - 2))
            O("Q", "dma_start", out=idx32[:, p * CW:p * CW + cw],
              in_=bass.AP(srch, rt * 128 * rowp + c0, [[rowp, 128], [1, cw]]),
              inc=16)
            CK("expl%d" % j, "Q")
            W("S", "expl%d" % j)
            if j >= 2:
                W("S", "expu%d" % (j - 2))
            O("S", "activation", out=idxb2[p][:, 0:cw],
              in_=idx32[:, p * CW:p * CW + cw], func=AF.Copy, inc=1)
            CK("expc%d" % j, "S")
            W("V", "expc%d" % j)
            if j >= 1:
                W("V", "expd%d" % (j - 1))
            ib = idxb2[p][:, 0:cw]

            def _m(l, dst):
                O("V", "tensor_scalar", out=dst[:, 0:cw], in0=ib,
                  scalar1=float(l), scalar2=lutb[:, l:l + 1],
                  op0=OP.is_equal, op1=OP.mult)

            def _add(dst, src):
                O("V", "tensor_add", out=dst[:, 0:cw], in0=dst[:, 0:cw],
                  in1=src[:, 0:cw])

            _m(0, acc2)
            _m(1, accB)
            _m(2, mrot[0])
            _m(3, mrot[1])
            _m(4, mrot[2])
            accs = [acc2, accB]
            for l in range(5, 16):
                _add(accs[l % 2], mrot[(l - 5) % 3])
                _m(l, mrot[(l - 5) % 3])
            _add(accs[0], mrot[2])
            _add(accs[1], mrot[0])
            _add(accs[0], mrot[1])
            O("V", "tensor_add", out=acc2[:, 0:cw], in0=acc2[:, 0:cw],
              in1=accB[:, 0:cw], inc=1)
            CK("expu%d" % j, "V")
            W("Q", "expu%d" % j)
            O("Q", "dma_start",
              out=bass.AP(dsth, rt * 128 * rowp + c0, [[rowp, 128], [1, cw]]),
              in_=acc2[:, 0:cw], inc=16)
            CK("expd%d" % j, "Q")
        ck_exp_done = "expd%d" % (len(exp_jobs) - 1)
        ck_exp_udone = "expu%d" % (len(exp_jobs) - 1)

        W("Q", ck_conv_cast_done)
        O("Q", "dma_start", out=xe_ap3(), in_=x_hbm_ap(0), inc=16)
        CK("xl0", "Q")

        W("G", ck_conv_done)

        for i in range(NT):
            warm = (i == 0)

            if i == 0:
                wS = ["xl0", ck_exp_udone]
                wT = ["lutb"]
            elif i == 1:
                wS = ["xl1", "vmm0_15"]
                wT = []
            else:
                wS = ["xl%d" % i, "outv%d" % (i - 1)]
                wT = ["outv%d" % (i - 1)]
            for h in range(HT):
                if h == 0:
                    W("S", *wS)
                O("S", "activation", out=kin(h), in_=xet(h), func=AF.Copy)
                O("S", "activation", out=vin(h), in_=xet(h), func=AF.Square, inc=1)
                CK("acast%d_%d" % (i, h), "S")
                if h == 0:
                    W("T", *wT)
                W("T", "acast%d_%d" % (i, h))
                O("T", "matmul", out=pb[6][0:1, :], lhsT=onesb[:, :], rhs=kin(h),
                  start=(h == 0), stop=(h == HT - 1))
                O("T", "matmul", out=pb[7][0:1, :], lhsT=onesb[:, :], rhs=vin(h),
                  start=(h == 0), stop=(h == HT - 1),
                  inc=(1 if h == HT - 1 else 0))
            CK("astat%d" % i, "T")
            W("V", "astat%d" % i)
            O("V", "tensor_scalar", out=mu, in0=pb[6][0:1, :], scalar1=1.0 / H,
              scalar2=None, op0=OP.mult)
            O("V", "tensor_scalar", out=msq, in0=pb[7][0:1, :], scalar1=1.0 / H,
              scalar2=None, op0=OP.mult)
            O("V", "tensor_mul", out=tsm, in0=mu, in1=mu)
            O("V", "tensor_sub", out=msq, in0=msq, in1=tsm, inc=1)
            CK("avar%d" % i, "V")
            W("S", "avar%d" % i)
            O("S", "activation", out=rstd, in_=msq, func=AF.Sqrt, bias=constsb[0:1, 0:1], inc=1)
            CK("asqr%d" % i, "S")
            W("V", "asqr%d" % i)
            O("V", "reciprocal", out=rstd, in_=rstd, inc=1)
            CK("arstd%d" % i, "V")
            W("T", "arstd%d" % i)
            O("T", "matmul", out=pb[6][:, :], lhsT=onesrow[:, :], rhs=mu,
              start=True, stop=True)
            O("T", "matmul", out=pb[7][:, :], lhsT=onesrow[:, :], rhs=rstd,
              start=True, stop=True, inc=1)
            CK("abc%d" % i, "T")
            W("V", "abc%d" % i)
            for g in range(4):
                for h in range(4 * g, 4 * g + 4):
                    O("V", "tensor_sub", out=ta, in0=xet(h), in1=pb[6][:, :])
                    O("V", "tensor_mul", out=xnt(h), in0=ta, in1=pb[7][:, :])
                for h in range(4 * g, 4 * g + 4):
                    O("V", "tensor_sub", out=ta, in0=xnt(h), in1=xnprev(h))
                    O("V", "scalar_tensor_tensor", out=kin(h), in0=ta,
                      scalar=tmkc[:, h:h + 1], in1=xnprev(h), op0=OP.mult,
                      op1=OP.add)
                    O("V", "scalar_tensor_tensor", out=vin(h), in0=ta,
                      scalar=tmvc[:, h:h + 1], in1=xnprev(h), op0=OP.mult,
                      op1=OP.add)
                    if not warm:
                        O("V", "scalar_tensor_tensor", out=rin(h), in0=ta,
                          scalar=tmrc[:, h:h + 1], in1=xnprev(h), op0=OP.mult,
                          op1=OP.add, inc=(1 if h == 4 * g + 3 else 0))
                    else:
                        if h == 4 * g + 3:
                            O("V", "engine_nop", inc=1)
                CK("amixg%d_%d" % (i, g), "V")
            O("V", "tensor_copy",
              out=bass.AP(xnb, 0, [[513 * HT, 128], [513, HT], [1, 1]]),
              in_=bass.AP(xnb, 512, [[513 * HT, 128], [513, HT], [1, 1]]), inc=1)
            CK("axn%d" % i, "V")
            CK("amix%d" % i, "V")

            for a in range(HT):
                pk = a % 2
                if a >= 2:
                    W("G", ("vmm%d_%d" if warm else "rmm%d_%d") % (i, a - 2))
                elif a == 0 and i == 1:
                    W("G", "vmm0_15")
                elif a == 0 and i >= 2:
                    W("G", "dmm%d_15" % (i - 1))
                O("G", "dma_start", out=kvrbuf(pk),
                  in_=bass.AP(ckvr, a * 384,
                              [[3 * H, 128], [128 * 3 * H, 16], [1, 384]]),
                  inc=16)
                CK("wkvr%d_%d" % (i, a), "G")
                if a >= 2:
                    W("T", ("scan%d_%d" if warm else "z%d_%d") % (i, a - 2))
                W("T", "wkvr%d_%d" % (i, a))
                for h in range(HT):
                    if a == 0 and h % 4 == 0:
                        W("T", "amixg%d_%d" % (i, h // 4))
                    O("T", "matmul", out=pb[0 + pk][:, :], lhsT=kvrtile(pk, h, 0),
                      rhs=kin(h), start=(h == 0), stop=(h == HT - 1),
                      inc=(1 if h == HT - 1 else 0))
                CK("kmm%d_%d" % (i, a), "T")
                for h in range(HT):
                    O("T", "matmul", out=pb[2 + pk][:, :], lhsT=kvrtile(pk, h, 1),
                      rhs=vin(h), start=(h == 0), stop=(h == HT - 1),
                      inc=(1 if h == HT - 1 else 0))
                CK("vmm%d_%d" % (i, a), "T")
                if not warm:
                    for h in range(HT):
                        O("T", "matmul", out=pb[4 + pk][:, :],
                          lhsT=kvrtile(pk, h, 2), rhs=rin(h),
                          start=(h == 0), stop=(h == HT - 1),
                          inc=(1 if h == HT - 1 else 0))
                    CK("rmm%d_%d" % (i, a), "T")
                W("S", "kmm%d_%d" % (i, a))
                if a >= 1:
                    W("S", "scan%d_%d" % (i, a - 1))
                O("S", "activation", out=ekb, in_=pb[0 + pk][:, :],
                  func=AF.Exp, inc=1)
                CK("ekx%d_%d" % (i, a), "S")
                if warm:
                    W("S", "ekx%d_%d" % (i, a))
                    O("S", "mul", out=ekb, in_=ekb, mul=maskc, inc=1)
                CK("ek%d_%d" % (i, a), "S")
                if not warm:
                    W("S", "rmm%d_%d" % (i, a))
                    O("S", "activation", out=pb[4 + pk][:, :], in_=pb[4 + pk][:, :],
                      func=AF.Sigmoid, inc=1)
                    CK("sig%d_%d" % (i, a), "S")
                O("V", "memset", ewb, 1.0)
                O("V", "tensor_scalar", out=ewb, in0=ewb,
                  scalar1=ewc[:, a:a + 1], scalar2=None, op0=OP.mult)
                W("V", "ek%d_%d" % (i, a), "vmm%d_%d" % (i, a))
                O("V", "tensor_mul", out=ekv, in0=ekb,
                  in1=pb[2 + pk][:, :])
                O("V", "tensor_copy", out=ns[:, 0:1], in_=nstate[:, a:a + 1])
                O("V", "tensor_copy", out=ds[:, 0:1], in_=dstate[:, a:a + 1])
                O("V", "tensor_tensor_scan", out=ns[:, 1:513], data0=ewb,
                  data1=ekv, initial=nstate[:, a:a + 1], op0=OP.mult, op1=OP.add)
                O("V", "tensor_tensor_scan", out=ds[:, 1:513], data0=ewb,
                  data1=ekb, initial=dstate[:, a:a + 1],
                  op0=OP.mult, op1=OP.add)
                O("V", "tensor_copy", out=nstate[:, a:a + 1], in_=ns[:, 512:513])
                O("V", "tensor_copy", out=dstate[:, a:a + 1], in_=ds[:, 512:513],
                  inc=1)
                CK("scan%d_%d" % (i, a), "V")
                if warm:
                    continue
                W("S", "scan%d_%d" % (i, a))
                O("S", "activation", out=ekv, in_=ekb, func=AF.Copy,
                  scale=etfc[:, a:a + 1], inc=1)
                CK("ektf%d_%d" % (i, a), "S")
                W("V", "ektf%d_%d" % (i, a))
                O("V", "tensor_mul", out=ta, in0=ekv, in1=pb[2 + pk][:, :])
                O("V", "tensor_add", out=ta, in0=ta, in1=ns[:, 0:512])
                O("V", "tensor_add", out=ns[:, 0:512], in0=ekv, in1=ds[:, 0:512])
                O("V", "reciprocal", out=ns[:, 0:512], in_=ns[:, 0:512])
                W("V", "sig%d_%d" % (i, a))
                O("V", "tensor_mul", out=ta, in0=ta, in1=ns[:, 0:512])
                O("V", "tensor_mul", out=zt(a), in0=ta, in1=pb[4 + pk][:, :],
                  inc=1)
                CK("z%d_%d" % (i, a), "V")

            if warm:
                W("Q", "axn%d" % i)
                O("Q", "dma_start", out=xe_ap3(), in_=x_hbm_ap(i + 1), inc=16)
                CK("xl%d" % (i + 1), "Q")
                continue

            for h in range(HT):
                ph = h % 2
                if h >= 2:
                    W("G", "ymm%d_%d" % (i, h - 2))
                O("G", "dma_start", out=ybuf_ap(ph),
                  in_=bass.AP(cwo, h * H * 128,
                              [[128, 128], [128 * 128, 16], [1, 128]]), inc=16)
                CK("wy%d_%d" % (i, h), "G")
                if h == 0:
                    W("T", "z%d_15" % i)
                if h >= 2:
                    W("T", "ybc%d_%d" % (i, h - 2))
                W("T", "wy%d_%d" % (i, h))
                for a in range(HT):
                    O("T", "matmul", out=pb[0 + ph][:, :], lhsT=ybtile(ph, a),
                      rhs=zt(a), start=(a == 0), stop=(a == HT - 1),
                      inc=(1 if a == HT - 1 else 0))
                CK("ymm%d_%d" % (i, h), "T")
                W("S", "ymm%d_%d" % (i, h))
                O("S", "activation", out=ybt(h), in_=pb[0 + ph][:, :], func=AF.Copy,
                  inc=1)
                CK("ybc%d_%d" % (i, h), "S")

            for h in range(HT):
                W("S", "ybc%d_%d" % (i, h))
                if h >= 2:
                    W("S", "bsm%d_%d" % (i, h - 2))
                O("S", "activation", out=sqb[h % 2], in_=ybt(h), func=AF.Square,
                  inc=1)
                CK("bsq%d_%d" % (i, h), "S")
                W("T", "bsq%d_%d" % (i, h))
                O("T", "matmul", out=pb[6][0:1, :], lhsT=onesb[:, :], rhs=ybt(h),
                  start=(h == 0), stop=(h == HT - 1))
                O("T", "matmul", out=pb[7][0:1, :], lhsT=onesb[:, :],
                  rhs=sqb[h % 2], start=(h == 0), stop=(h == HT - 1), inc=1)
                CK("bsm%d_%d" % (i, h), "T")
            CK("bstat%d" % i, "T")
            W("V", "bstat%d" % i)
            O("V", "tensor_scalar", out=mu, in0=pb[6][0:1, :], scalar1=1.0 / H,
              scalar2=None, op0=OP.mult)
            O("V", "tensor_scalar", out=msq, in0=pb[7][0:1, :], scalar1=1.0 / H,
              scalar2=None, op0=OP.mult)
            O("V", "tensor_mul", out=tsm, in0=mu, in1=mu)
            O("V", "tensor_sub", out=msq, in0=msq, in1=tsm, inc=1)
            CK("bvar%d" % i, "V")
            W("S", "bvar%d" % i)
            O("S", "activation", out=rstd, in_=msq, func=AF.Sqrt, bias=constsb[0:1, 0:1], inc=1)
            CK("bsqr%d" % i, "S")
            W("V", "bsqr%d" % i)
            O("V", "reciprocal", out=rstd, in_=rstd, inc=1)
            CK("brstd%d" % i, "V")
            W("T", "brstd%d" % i)
            O("T", "matmul", out=pb[6][:, :], lhsT=onesrow[:, :], rhs=mu,
              start=True, stop=True)
            O("T", "matmul", out=pb[7][:, :], lhsT=onesrow[:, :], rhs=rstd,
              start=True, stop=True, inc=1)
            CK("bbc%d" % i, "T")
            W("V", "bbc%d" % i)
            for h in range(HT):
                O("V", "tensor_sub", out=ta, in0=ybt(h), in1=pb[6][:, :])
                O("V", "tensor_mul", out=ta, in0=ta, in1=pb[7][:, :])
                O("V", "tensor_add", out=xet(h), in0=xet(h), in1=ta,
                  inc=(1 if h == HT - 1 else 0))
            CK("x1%d" % i, "V")

            for h in range(HT):
                if h == 0:
                    W("S", "x1%d" % i)
                if h >= 2:
                    W("S", "csm%d_%d" % (i, h - 2))
                O("S", "activation", out=rin(h), in_=xet(h), func=AF.Copy)
                O("S", "activation", out=sqb[h % 2], in_=xet(h), func=AF.Square,
                  inc=1)
                CK("csq%d_%d" % (i, h), "S")
                if h == 0:
                    W("T", "x1%d" % i)
                W("T", "csq%d_%d" % (i, h))
                O("T", "matmul", out=pb[6][0:1, :], lhsT=onesb[:, :], rhs=rin(h),
                  start=(h == 0), stop=(h == HT - 1))
                O("T", "matmul", out=pb[7][0:1, :], lhsT=onesb[:, :],
                  rhs=sqb[h % 2], start=(h == 0), stop=(h == HT - 1), inc=1)
                CK("csm%d_%d" % (i, h), "T")
            CK("cstat%d" % i, "T")
            W("V", "cstat%d" % i)
            O("V", "tensor_scalar", out=mu, in0=pb[6][0:1, :], scalar1=1.0 / H,
              scalar2=None, op0=OP.mult)
            O("V", "tensor_scalar", out=msq, in0=pb[7][0:1, :], scalar1=1.0 / H,
              scalar2=None, op0=OP.mult)
            O("V", "tensor_mul", out=tsm, in0=mu, in1=mu)
            O("V", "tensor_sub", out=msq, in0=msq, in1=tsm, inc=1)
            CK("cvar%d" % i, "V")
            W("S", "cvar%d" % i)
            O("S", "activation", out=rstd, in_=msq, func=AF.Sqrt, bias=constsb[0:1, 0:1], inc=1)
            CK("csqr%d" % i, "S")
            W("V", "csqr%d" % i)
            O("V", "reciprocal", out=rstd, in_=rstd, inc=1)
            CK("crstd%d" % i, "V")
            W("T", "crstd%d" % i)
            O("T", "matmul", out=pb[6][:, :], lhsT=onesrow[:, :], rhs=mu,
              start=True, stop=True)
            O("T", "matmul", out=pb[7][:, :], lhsT=onesrow[:, :], rhs=rstd,
              start=True, stop=True, inc=1)
            CK("cbc%d" % i, "T")
            W("V", "cbc%d" % i)
            for h in range(HT):
                O("V", "tensor_sub", out=ta, in0=xet(h), in1=pb[6][:, :])
                O("V", "tensor_mul", out=x2n(h), in0=ta, in1=pb[7][:, :],
                  inc=(1 if h == HT - 1 else 0))
            CK("x2n%d" % i, "V")

            for o in range(OT):
                po = o % 2
                if o < 2:
                    W("Q", ck_exp_done, "rmm%d_15" % i)
                else:
                    W("Q", "gumm%d_%d" % (i, o - 2))
                O("Q", "dma_start", out=gubuf_ap(po),
                  in_=bass.AP(cwgu, o * 256,
                              [[GUW, 128], [128 * GUW, 16], [1, 256]]), inc=16)
                CK("wgu%d_%d" % (i, o), "Q")
                if o == 0:
                    W("T", "x2n%d" % i)
                if o >= 2:
                    W("T", "hact%d_%d" % (i, o - 2))
                W("T", "wgu%d_%d" % (i, o))
                for h in range(HT):
                    O("T", "matmul", out=pb[0 + po][:, :], lhsT=gutile(po, h, 0),
                      rhs=x2n(h), start=(h == 0), stop=(h == HT - 1),
                      inc=(1 if h == HT - 1 else 0))
                CK("gmm%d_%d" % (i, o), "T")
                for h in range(HT):
                    O("T", "matmul", out=pb[2 + po][:, :], lhsT=gutile(po, h, 1),
                      rhs=x2n(h), start=(h == 0), stop=(h == HT - 1),
                      inc=(1 if h == HT - 1 else 0))
                CK("gumm%d_%d" % (i, o), "T")
                W("S", "gmm%d_%d" % (i, o))
                O("S", "activation", out=sqb[o % 2], in_=pb[0 + po][:, :],
                  func=AF.Silu, inc=1)
                CK("silu%d_%d" % (i, o), "S")
                W("V", "silu%d_%d" % (i, o), "gumm%d_%d" % (i, o))
                O("V", "tensor_mul", out=hact(o), in0=sqb[o % 2],
                  in1=pb[2 + po][:, :], inc=1)
                CK("hact%d_%d" % (i, o), "V")

            for h in range(HT):
                ph = h % 2
                if h == 0:
                    W("G", "gumm%d_%d" % (i, OT - 1))
                if h >= 2:
                    W("G", "dmm%d_%d" % (i, h - 2))
                O("G", "dma_start", out=dbuf_ap(ph),
                  in_=bass.AP(cwd, h * 128, [[H, 128], [128 * H, 43], [1, 128]]),
                  inc=16)
                CK("wd%d_%d" % (i, h), "G")
                if h == 0:
                    W("T", "hact%d_42" % i)
                if h >= 2:
                    W("T", "y2c%d_%d" % (i, h - 2))
                W("T", "wd%d_%d" % (i, h))
                for o in range(OT):
                    O("T", "matmul", out=pb[4 + ph][:, :], lhsT=dtile(ph, o),
                      rhs=hact(o), start=(o == 0), stop=(o == OT - 1),
                      inc=(1 if o == OT - 1 else 0))
                CK("dmm%d_%d" % (i, h), "T")
                W("S", "dmm%d_%d" % (i, h))
                O("S", "activation", out=ybt(h), in_=pb[4 + ph][:, :],
                  func=AF.Copy, inc=1)
                CK("y2c%d_%d" % (i, h), "S")

            for h in range(HT):
                W("S", "y2c%d_%d" % (i, h))
                if h >= 2:
                    W("S", "dsm%d_%d" % (i, h - 2))
                O("S", "activation", out=sqb[h % 2], in_=ybt(h), func=AF.Square,
                  inc=1)
                CK("dsq%d_%d" % (i, h), "S")
                if h == 0:
                    W("T", "x2n%d" % i)
                W("T", "dsq%d_%d" % (i, h))
                O("T", "matmul", out=pb[6][0:1, :], lhsT=onesb[:, :], rhs=ybt(h),
                  start=(h == 0), stop=(h == HT - 1))
                O("T", "matmul", out=pb[7][0:1, :], lhsT=onesb[:, :],
                  rhs=sqb[h % 2], start=(h == 0), stop=(h == HT - 1), inc=1)
                CK("dsm%d_%d" % (i, h), "T")
            CK("dstat%d" % i, "T")
            W("V", "dstat%d" % i)
            O("V", "tensor_scalar", out=mu, in0=pb[6][0:1, :], scalar1=1.0 / H,
              scalar2=None, op0=OP.mult)
            O("V", "tensor_scalar", out=msq, in0=pb[7][0:1, :], scalar1=1.0 / H,
              scalar2=None, op0=OP.mult)
            O("V", "tensor_mul", out=tsm, in0=mu, in1=mu)
            O("V", "tensor_sub", out=msq, in0=msq, in1=tsm, inc=1)
            CK("dvar%d" % i, "V")
            W("S", "dvar%d" % i)
            O("S", "activation", out=rstd, in_=msq, func=AF.Sqrt, bias=constsb[0:1, 0:1], inc=1)
            CK("dsqr%d" % i, "S")
            W("V", "dsqr%d" % i)
            O("V", "reciprocal", out=rstd, in_=rstd, inc=1)
            CK("drstd%d" % i, "V")
            W("T", "drstd%d" % i)
            O("T", "matmul", out=pb[6][:, :], lhsT=onesrow[:, :], rhs=mu,
              start=True, stop=True)
            O("T", "matmul", out=pb[7][:, :], lhsT=onesrow[:, :], rhs=rstd,
              start=True, stop=True, inc=1)
            CK("dbc%d" % i, "T")
            W("V", "dbc%d" % i)
            for h in range(HT):
                O("V", "tensor_sub", out=ta, in0=ybt(h), in1=pb[6][:, :])
                O("V", "tensor_mul", out=ta, in0=ta, in1=pb[7][:, :])
                O("V", "tensor_add", out=xet(h), in0=xet(h), in1=ta,
                  inc=(1 if h == HT - 1 else 0))
            CK("outv%d" % i, "V")

            W("Q", "outv%d" % i)
            O("Q", "dma_start", out=y_hbm_ap(i - 1), in_=xe_ap3(), inc=16)
            CK("outd%d" % i, "Q")
            if i + 1 < NT:
                W("Q", "outd%d" % i)
                O("Q", "dma_start", out=xe_ap3(), in_=x_hbm_ap(i + 1), inc=16)
                CK("xl%d" % (i + 1), "Q")

        if ctx.mode == "dry":
            ctx.ck["END_Q"] = [("Q" + str(r), ctx.cnt["Q" + str(r)])
                               for r in range(NRING)
                               if ctx.cnt["Q" + str(r)] > 0]
        if ctx.mode == "Q":
            for src, val in ctx.ck["END_Q"]:
                ctx.handles["Q"].wait_ge(ctx.sems[src], val)

    dry = Ctx("dry")
    script(dry)

    with es:
        with ExitStack() as ses:
            sems = {}
            for k in "TVS":
                sems[k] = ses.enter_context(nc.semaphore("s" + k))
            for q in "QG":
                for r in range(NRING):
                    sems[q + str(r)] = ses.enter_context(
                        nc.semaphore("s%s%d" % (q, r)))
            block = ses.enter_context(nc.Block())

            @block.tensor
            def _(eng):
                script(Ctx("T", {"T": eng}, sems, dry.ck))

            @block.vector
            def _(eng):
                script(Ctx("V", {"V": eng}, sems, dry.ck))

            @block.scalar
            def _(eng):
                script(Ctx("S", {"S": eng}, sems, dry.ck))

            @block.sync
            def _(eng):
                script(Ctx("Q", {"Q": eng}, sems, dry.ck))

            @block.gpsimd
            def _(eng):
                script(Ctx("G", {"G": eng}, sems, dry.ck))

    return nc


def kernel(L0_attn_hidden, L0_attn_time_decay, L0_attn_time_first,
           L0_attn_time_mix_key, L0_attn_time_mix_value,
           L0_attn_time_mix_receptance, L0_attn_W_key, L0_attn_W_value,
           L0_attn_W_receptance, L0_attn_W_output, L0_ffn_lut,
           L0_ffn_gate_idx, L0_ffn_up_idx, L0_ffn_down_idx):
    B, T, _ = L0_attn_hidden.shape

    wkT = np.ascontiguousarray(np.asarray(L0_attn_W_key, np.float32).T)
    wvT = np.ascontiguousarray(np.asarray(L0_attn_W_value, np.float32).T)
    wrT = np.ascontiguousarray(np.asarray(L0_attn_W_receptance, np.float32).T)
    woT = np.ascontiguousarray(np.asarray(L0_attn_W_output, np.float32).T)

    g = np.asarray(L0_ffn_gate_idx, np.int32).T.reshape(H, OT, 128)
    u = np.asarray(L0_ffn_up_idx, np.int32).T.reshape(H, OT, 128)
    giuT = np.ascontiguousarray(np.stack([g, u], axis=2).reshape(H, GUW))
    didT = np.ascontiguousarray(np.asarray(L0_ffn_down_idx, np.int32).T)

    lutrow = np.ascontiguousarray(
        np.asarray(L0_ffn_lut, np.float32).reshape(1, 16))

    def colsify(v):
        return np.asarray(v, np.float32).reshape(HT, 128).T

    x_full = np.asarray(L0_attn_hidden, np.float32)

    in_maps = []
    for c in range(8):
        b, q = c // 4, c % 4
        t0 = q * TC
        xs = np.zeros((XW, H), np.float32)
        if q == 0:
            xs[512:] = x_full[b, 0:TC]
        else:
            xs[:] = x_full[b, t0 - 512:t0 + TC]
        x_fm = np.ascontiguousarray(xs.T)

        cols = np.zeros((128, 81), np.float32)
        cols[:, 0:16] = colsify(L0_attn_time_decay)
        cols[:, 16:32] = colsify(L0_attn_time_first)
        cols[:, 32:48] = colsify(np.asarray(L0_attn_time_mix_key).reshape(-1))
        cols[:, 48:64] = colsify(np.asarray(L0_attn_time_mix_value).reshape(-1))
        cols[:, 64:80] = colsify(
            np.asarray(L0_attn_time_mix_receptance).reshape(-1))
        cols[:, 80] = 0.0 if q == 0 else 1.0

        in_maps.append({
            "x": x_fm, "wkT": wkT, "wvT": wvT, "wrT": wrT, "woT": woT,
            "giuT": giuT, "didT": didT, "colsin": cols, "lutrow": lutrow,
        })

    nc = build_program()
    res = run_bass_kernel_spmd(nc, in_maps, core_ids=list(range(8)))

    out = np.empty((B, T, H), np.float32)
    for c in range(8):
        b, q = c // 4, c % 4
        out[b, q * TC:(q + 1) * TC, :] = res.results[c]["y"].T
    return out


# version 14 (final)
# speedup vs baseline: 1.0726x; 1.0014x over previous
import numpy as np
import concourse.bass as bass
import concourse.mybir as mybir
from concourse.bass_utils import run_bass_kernel_spmd

F32 = mybir.dt.float32
BF16 = mybir.dt.bfloat16
I32 = mybir.dt.int32
AF = mybir.ActivationFunctionType
OP = mybir.AluOpType
PM = mybir.MatmulPerfMode.DoubleRow

H = 2048
HT = 16
TT = 512
NT = 5
TC = 2048
XW = 512 + TC
INTER = 5504
OT = 43
GUW = 2 * INTER
CW = 2048


NRING = 8
HAZ_WIN = 4

_IN_KEYS = ("in_", "in0", "in1", "data0", "data1", "initial", "scalar",
            "scalar1", "scalar2", "scale", "bias", "mul", "mask",
            "on_true", "on_false", "lhsT", "rhs", "weights")


def _ap_region(v):
    try:
        t = v.tensor
        name = t.name
        off = int(v.offset)
        span = 1
        for pair in v.ap:
            st, n = int(pair[0]), int(pair[1])
            span += st * (n - 1)
        return (name, off, off + span)
    except Exception:
        return None


class Ctx:

    def __init__(self, mode, handles=None, sems=None, ck=None):
        self.mode = mode
        self.handles = handles or {}
        self.sems = sems or {}
        self.ck = ck if ck is not None else {}
        self.cnt = {}
        for k in "TVS":
            self.cnt[k] = 0
        for q in "QG":
            for r in range(NRING):
                self.cnt[q + str(r)] = 0
        self.dmaj = {"Q": 0, "G": 0}
        self.lastw = {k: [] for k in "TVS"}
        self.hiwater = {}

    def op(self, eng, name, *args, inc=0, **kw):
        if name == "dma_start":
            j = self.dmaj[eng]
            self.dmaj[eng] = j + 1
            key = eng + str(j % NRING)
            if self.mode == "dry":
                self.cnt[key] = 16 * (j // NRING + 1)
                return
            if self.mode != eng:
                return
            if j >= NRING and 16 * (j // NRING) > self.hiwater.get(key, 0):
                self.hiwater[key] = 16 * (j // NRING)
                self.handles[eng].wait_ge(self.sems[key], 16 * (j // NRING))
            ins = getattr(self.handles[eng], name)(*args, **kw)
            ins.then_inc(self.sems[key], 16)
            return
        ins_regions = []
        out_regions = []
        if name == "memset":
            r = _ap_region(args[0])
            if r:
                out_regions.append(r)
        else:
            for k, v in kw.items():
                r = _ap_region(v)
                if r is None:
                    continue
                if k in ("out", "accum_out"):
                    out_regions.append(r)
                elif k in _IN_KEYS:
                    ins_regions.append(r)
        def _ovl(a, b):
            for (an, aa, ab) in a:
                for (bn, ba, bb) in b:
                    if an == bn and aa < bb and ba < ab:
                        return True
            return False

        need = 0
        for wval, wregs, rregs in self.lastw[eng]:
            if (_ovl(wregs, ins_regions) or _ovl(rregs, out_regions)
                    or _ovl(wregs, out_regions)):
                need = max(need, wval)
        myval = self.cnt[eng] + 1
        self.cnt[eng] = myval
        self.lastw[eng].append((myval, out_regions, ins_regions))
        if len(self.lastw[eng]) > HAZ_WIN:
            self.lastw[eng].pop(0)
        if self.mode == "dry":
            return
        if self.mode != eng:
            return
        if need and need > self.hiwater.get(eng, 0):
            self.hiwater[eng] = need
            self.handles[eng].wait_ge(self.sems[eng], need)
        ins = getattr(self.handles[eng], name)(*args, **kw)
        ins.then_inc(self.sems[eng], 1)

    def ckpt(self, name, eng):
        if self.mode == "dry":
            assert name not in self.ck, name
            if eng in ("Q", "G"):
                self.ck[name] = [(eng + str(r), self.cnt[eng + str(r)])
                                 for r in range(NRING)
                                 if self.cnt[eng + str(r)] > 0]
            else:
                self.ck[name] = [(eng, self.cnt[eng])]

    def wait(self, eng, *names):
        if self.mode == "dry":
            for n in names:
                assert n in self.ck, n
            return
        if self.mode != eng:
            return
        for n in names:
            for src, val in self.ck[n]:
                if val > 0 and val > self.hiwater.get(src, 0):
                    self.hiwater[src] = val
                    self.handles[eng].wait_ge(self.sems[src], val)


def build_program():
    nc = bass.Bass("TRN2", target_bir_lowering=False)

    x_h = nc.dram_tensor("x", [H, XW], F32, kind="ExternalInput")
    wk_h = nc.dram_tensor("wkT", [H, H], F32, kind="ExternalInput")
    wv_h = nc.dram_tensor("wvT", [H, H], F32, kind="ExternalInput")
    wr_h = nc.dram_tensor("wrT", [H, H], F32, kind="ExternalInput")
    wo_h = nc.dram_tensor("woT", [H, H], F32, kind="ExternalInput")
    giu_h = nc.dram_tensor("giuT", [H, GUW], I32, kind="ExternalInput")
    did_h = nc.dram_tensor("didT", [INTER, H], I32, kind="ExternalInput")
    cols_h = nc.dram_tensor("colsin", [128, 81], F32, kind="ExternalInput")
    lut_h = nc.dram_tensor("lutrow", [1, 16], F32, kind="ExternalInput")
    y_h = nc.dram_tensor("y", [H, TC], F32, kind="ExternalOutput")

    ckvr = nc.dram_tensor("ckvr", [H, 3 * H], BF16, kind="Internal")
    cwo = nc.dram_tensor("cwo", [HT * H, 128], BF16, kind="Internal")
    cwgu = nc.dram_tensor("cwgu", [H, GUW], BF16, kind="Internal")
    cwd = nc.dram_tensor("cwd", [INTER, H], BF16, kind="Internal")

    from contextlib import ExitStack
    es = ExitStack()

    def sbt(name, shape, dt):
        return es.enter_context(nc.sbuf_tensor(name, shape, dt))

    xe = sbt("xe", [128, HT * TT], F32)
    xnb = sbt("xnb", [128, HT * 513], BF16)
    pool = sbt("pool", [128, 30208], BF16)
    zb = sbt("zb", [128, 8192], BF16)
    wbuf = sbt("wbuf", [128, 17408], BF16)
    idx32 = sbt("idx32", [128, 2 * CW], I32)
    trans = sbt("trans", [128, 3074], F32)
    colsb = sbt("colsb", [128, 81], F32)
    colsd = sbt("colsd", [128, 48], F32)
    lutb = sbt("lutb", [128, 16], F32)
    nstate = sbt("nstate", [128, 16], F32)
    dstate = sbt("dstate", [128, 16], F32)
    onesrow = sbt("onesrow", [1, 128], F32)
    onesb = sbt("onesb", [128, 1], BF16)
    constsb = sbt("constsb", [128, 2], F32)
    nc.const_aps.register(F32, 1e-5, constsb[:, 0:1])
    nc.const_aps.register(F32, -1.0, constsb[:, 1:2])

    pb = [es.enter_context(nc.psum_tensor("pb%d" % i, [128, 512], F32))
          for i in range(8)]

    def kin(h):
        return pool[:, h * TT:(h + 1) * TT]

    def vin(h):
        return pool[:, 8192 + h * TT: 8192 + (h + 1) * TT]

    def rin(h):
        return pool[:, 16384 + h * TT: 16384 + (h + 1) * TT]

    def ybt(h):
        return pool[:, h * TT:(h + 1) * TT]

    def hact(o):
        return pool[:, 8192 + o * TT: 8192 + (o + 1) * TT]

    def xet(h):
        return xe[:, h * TT:(h + 1) * TT]

    def xnt(h):
        return xnb[:, h * 513 + 1: h * 513 + 513]

    def xnprev(h):
        return xnb[:, h * 513: h * 513 + 512]

    def zt(a):
        return zb[:, a * TT:(a + 1) * TT]

    def x2n(h):
        return zb[:, h * TT:(h + 1) * TT]

    ekv = trans[:, 0:512]
    ta = trans[:, 512:1024]
    ewb = trans[:, 1024:1536]
    ns = trans[:, 1536:2049]
    ds = trans[:, 2049:2562]
    ekb = trans[:, 2562:3074]

    ewc = colsd[:, 0:16]
    etfc = colsd[:, 16:32]
    ttmp = colsd[:, 32:48]
    tdc = colsb[:, 0:16]
    tfc = colsb[:, 16:32]
    tmkc = colsb[:, 32:48]
    tmvc = colsb[:, 48:64]
    tmrc = colsb[:, 64:80]
    maskc = colsb[:, 80:81]

    mu = ds[0:1, 0:512]
    msq = ta[0:1, 0:512]
    tsm = ewb[0:1, 0:512]
    rstd = ns[0:1, 0:512]

    def kvrbuf(p):
        return bass.AP(wbuf, p * 6144, [[17408, 128], [384, 16], [1, 384]])

    def kvrtile(p, h, m):
        return wbuf[:, p * 6144 + h * 384 + m * 128: p * 6144 + h * 384 + (m + 1) * 128]

    def ybuf_ap(p):
        return bass.AP(wbuf, 12288 + p * 2048, [[17408, 128], [128, 16], [1, 128]])

    def ybtile(p, a):
        return wbuf[:, 12288 + p * 2048 + a * 128: 12288 + p * 2048 + (a + 1) * 128]

    def gubuf_ap(p):
        return bass.AP(wbuf, p * 4096, [[17408, 128], [256, 16], [1, 256]])

    def gutile(p, h, m):
        return wbuf[:, p * 4096 + h * 256 + m * 128: p * 4096 + h * 256 + (m + 1) * 128]

    def dbuf_ap(p):
        return bass.AP(wbuf, p * 5504, [[17408, 128], [128, 43], [1, 128]])

    def dtile(p, o):
        return wbuf[:, p * 5504 + o * 128: p * 5504 + (o + 1) * 128]

    sqb = [wbuf[:, 16384:16896], wbuf[:, 16896:17408]]

    idxb2 = [zb[:, 0:CW], zb[:, CW:2 * CW]]
    acc2 = zb[:, 2 * CW:3 * CW]
    accB = zb[:, 3 * CW:4 * CW]
    mrot = [pool[:, 0:CW], pool[:, CW:2 * CW], pool[:, 2 * CW:3 * CW]]

    def xe_ap3():
        return bass.AP(xe, 0, [[HT * TT, 128], [TT, HT], [1, TT]])

    def x_hbm_ap(t):
        return bass.AP(x_h, TT * t, [[XW, 128], [128 * XW, HT], [1, TT]])

    def y_hbm_ap(t):
        return bass.AP(y_h, TT * t, [[TC, 128], [128 * TC, HT], [1, TT]])

    def script(ctx):
        W = ctx.wait
        O = ctx.op
        CK = ctx.ckpt

        O("Q", "dma_start", out=colsb[:, :], in_=cols_h[:, :], inc=16)
        O("Q", "dma_start", out=lutb[0:1, :], in_=lut_h[0:1, :], inc=16)
        CK("cols", "Q")
        O("V", "memset", onesrow[:, :], 1.0)
        O("V", "memset", onesb[:, :], 1.0)
        O("V", "memset", constsb[:, 0:1], 1e-5)
        O("V", "memset", constsb[:, 1:2], -1.0)
        O("V", "memset", nstate[:, :], 0.0)
        O("V", "memset", dstate[:, :], 0.0)
        O("V", "memset", bass.AP(xnb, 0, [[513 * HT, 128], [513, HT], [1, 1]]), 0.0,
          inc=1)
        CK("vinit", "V")
        W("T", "cols", "vinit")
        O("T", "matmul", out=pb[6][:, 0:16], lhsT=onesrow[:, :], rhs=lutb[0:1, :],
          start=True, stop=True, inc=1)
        CK("lutmm", "T")
        W("V", "lutmm")
        O("V", "tensor_copy", out=lutb[:, :], in_=pb[6][:, 0:16], inc=1)
        CK("lutb", "V")
        W("S", "cols", "vinit")
        O("S", "activation", out=ttmp, in_=tdc, func=AF.Exp, inc=1)
        CK("ttmpd", "S")
        W("S", "ttmpd")
        O("S", "activation", out=ewc, in_=ttmp, func=AF.Exp, scale=constsb[:, 1:2])
        O("S", "activation", out=etfc, in_=tfc, func=AF.Exp, inc=1)
        CK("chancols", "S")

        conv_jobs = []
        for m, wh in enumerate((wk_h, wv_h, wr_h)):
            for t in range(HT):
                conv_jobs.append(("kvr", m, t, wh))
        for t in range(HT):
            conv_jobs.append(("wo", 0, t, wo_h))
        for j, (kind, m, t, wh) in enumerate(conv_jobs):
            p = j % 2
            src32 = xe[:, p * 2048:(p + 1) * 2048]
            dst16 = wbuf[:, p * 2048:(p + 1) * 2048]
            if j >= 2:
                W("G", "convc%d" % (j - 2))
                W("G", "convs%d" % (j - 2))
            O("G", "dma_start", out=src32,
              in_=bass.AP(wh, t * 128 * H, [[H, 128], [1, 2048]]), inc=16)
            CK("convl%d" % j, "G")
            W("S", "convl%d" % j)
            O("S", "activation", out=dst16, in_=src32, func=AF.Copy, inc=1)
            CK("convc%d" % j, "S")
            W("G", "convc%d" % j)
            if kind == "kvr":
                dap = bass.AP(ckvr, (t * 128) * (3 * H) + m * 128,
                              [[3 * H, 128], [384, 16], [1, 128]])
            else:
                dap = bass.AP(cwo, (t * 128) * 128,
                              [[128, 128], [H * 128, 16], [1, 128]])
            O("G", "dma_start", out=dap,
              in_=bass.AP(wbuf, p * 2048, [[17408, 128], [128, 16], [1, 128]]),
              inc=16)
            CK("convs%d" % j, "G")
        ck_conv_done = "convs%d" % (len(conv_jobs) - 1)
        ck_conv_cast_done = "convc%d" % (len(conv_jobs) - 1)

        exp_jobs = []
        for t in range(HT):
            c0 = 0
            while c0 < GUW:
                cw = min(CW, GUW - c0)
                exp_jobs.append((giu_h, cwgu, t, c0, cw, GUW))
                c0 += cw
        for t in range(OT):
            exp_jobs.append((did_h, cwd, t, 0, H, H))
        W("V", "lutb")
        for j, (srch, dsth, rt, c0, cw, rowp) in enumerate(exp_jobs):
            p = j % 2
            if j >= 2:
                W("Q", "expc%d" % (j - 2))
            O("Q", "dma_start", out=idx32[:, p * CW:p * CW + cw],
              in_=bass.AP(srch, rt * 128 * rowp + c0, [[rowp, 128], [1, cw]]),
              inc=16)
            CK("expl%d" % j, "Q")
            W("S", "expl%d" % j)
            if j >= 2:
                W("S", "expu%d" % (j - 2))
            O("S", "activation", out=idxb2[p][:, 0:cw],
              in_=idx32[:, p * CW:p * CW + cw], func=AF.Copy, inc=1)
            CK("expc%d" % j, "S")
            W("V", "expc%d" % j)
            if j >= 1:
                W("V", "expd%d" % (j - 1))
            ib = idxb2[p][:, 0:cw]

            def _m(l, dst):
                O("V", "tensor_scalar", out=dst[:, 0:cw], in0=ib,
                  scalar1=float(l), scalar2=lutb[:, l:l + 1],
                  op0=OP.is_equal, op1=OP.mult)

            def _add(dst, src):
                O("V", "tensor_add", out=dst[:, 0:cw], in0=dst[:, 0:cw],
                  in1=src[:, 0:cw])

            _m(0, acc2)
            _m(1, accB)
            _m(2, mrot[0])
            _m(3, mrot[1])
            _m(4, mrot[2])
            accs = [acc2, accB]
            for l in range(5, 16):
                _add(accs[l % 2], mrot[(l - 5) % 3])
                _m(l, mrot[(l - 5) % 3])
            _add(accs[0], mrot[2])
            _add(accs[1], mrot[0])
            _add(accs[0], mrot[1])
            O("V", "tensor_add", out=acc2[:, 0:cw], in0=acc2[:, 0:cw],
              in1=accB[:, 0:cw], inc=1)
            CK("expu%d" % j, "V")
            W("Q", "expu%d" % j)
            O("Q", "dma_start",
              out=bass.AP(dsth, rt * 128 * rowp + c0, [[rowp, 128], [1, cw]]),
              in_=acc2[:, 0:cw], inc=16)
            CK("expd%d" % j, "Q")
        ck_exp_done = "expd%d" % (len(exp_jobs) - 1)
        ck_exp_udone = "expu%d" % (len(exp_jobs) - 1)

        W("Q", ck_conv_cast_done)
        O("Q", "dma_start", out=xe_ap3(), in_=x_hbm_ap(0), inc=16)
        CK("xl0", "Q")

        W("G", ck_conv_done)

        for i in range(NT):
            warm = (i == 0)

            if i == 0:
                wS = ["xl0", ck_exp_udone]
                wT = ["lutb"]
            elif i == 1:
                wS = ["xl1", "vmm0_15"]
                wT = []
            else:
                wS = ["xl%d" % i, "outv%d" % (i - 1)]
                wT = ["outv%d" % (i - 1)]
            for h in range(HT):
                if h == 0:
                    W("S", *wS)
                O("S", "activation", out=kin(h), in_=xet(h), func=AF.Copy)
                O("S", "activation", out=vin(h), in_=xet(h), func=AF.Square, inc=1)
                CK("acast%d_%d" % (i, h), "S")
                if h == 0:
                    W("T", *wT)
                W("T", "acast%d_%d" % (i, h))
                O("T", "matmul", out=pb[6][0:1, :], lhsT=onesb[:, :], rhs=kin(h),
                  start=(h == 0), stop=(h == HT - 1))
                O("T", "matmul", out=pb[7][0:1, :], lhsT=onesb[:, :], rhs=vin(h),
                  start=(h == 0), stop=(h == HT - 1),
                  inc=(1 if h == HT - 1 else 0))
            CK("astat%d" % i, "T")
            W("V", "astat%d" % i)
            O("V", "tensor_scalar", out=mu, in0=pb[6][0:1, :], scalar1=1.0 / H,
              scalar2=None, op0=OP.mult)
            O("V", "tensor_scalar", out=msq, in0=pb[7][0:1, :], scalar1=1.0 / H,
              scalar2=None, op0=OP.mult)
            O("V", "tensor_mul", out=tsm, in0=mu, in1=mu)
            O("V", "tensor_sub", out=msq, in0=msq, in1=tsm, inc=1)
            CK("avar%d" % i, "V")
            W("S", "avar%d" % i)
            O("S", "activation", out=rstd, in_=msq, func=AF.Sqrt, bias=constsb[0:1, 0:1], inc=1)
            CK("asqr%d" % i, "S")
            W("V", "asqr%d" % i)
            O("V", "reciprocal", out=rstd, in_=rstd, inc=1)
            CK("arstd%d" % i, "V")
            W("T", "arstd%d" % i)
            O("T", "matmul", out=pb[6][:, :], lhsT=onesrow[:, :], rhs=mu,
              start=True, stop=True)
            O("T", "matmul", out=pb[7][:, :], lhsT=onesrow[:, :], rhs=rstd,
              start=True, stop=True, inc=1)
            CK("abc%d" % i, "T")
            W("V", "abc%d" % i)
            for g in range(4):
                for h in range(4 * g, 4 * g + 4):
                    O("V", "tensor_sub", out=ta, in0=xet(h), in1=pb[6][:, :])
                    O("V", "tensor_mul", out=xnt(h), in0=ta, in1=pb[7][:, :])
                for h in range(4 * g, 4 * g + 4):
                    O("V", "tensor_sub", out=ta, in0=xnt(h), in1=xnprev(h))
                    O("V", "scalar_tensor_tensor", out=kin(h), in0=ta,
                      scalar=tmkc[:, h:h + 1], in1=xnprev(h), op0=OP.mult,
                      op1=OP.add)
                    O("V", "scalar_tensor_tensor", out=vin(h), in0=ta,
                      scalar=tmvc[:, h:h + 1], in1=xnprev(h), op0=OP.mult,
                      op1=OP.add)
                    if not warm:
                        O("V", "scalar_tensor_tensor", out=rin(h), in0=ta,
                          scalar=tmrc[:, h:h + 1], in1=xnprev(h), op0=OP.mult,
                          op1=OP.add, inc=(1 if h == 4 * g + 3 else 0))
                    else:
                        if h == 4 * g + 3:
                            O("V", "engine_nop", inc=1)
                CK("amixg%d_%d" % (i, g), "V")
            O("V", "tensor_copy",
              out=bass.AP(xnb, 0, [[513 * HT, 128], [513, HT], [1, 1]]),
              in_=bass.AP(xnb, 512, [[513 * HT, 128], [513, HT], [1, 1]]), inc=1)
            CK("axn%d" % i, "V")
            CK("amix%d" % i, "V")

            for a in range(HT):
                pk = a % 2
                if a >= 2:
                    W("G", ("vmm%d_%d" if warm else "rmm%d_%d") % (i, a - 2))
                elif a == 0 and i == 1:
                    W("G", "vmm0_15")
                elif a == 0 and i >= 2:
                    W("G", "dmm%d_15" % (i - 1))
                O("G", "dma_start", out=kvrbuf(pk),
                  in_=bass.AP(ckvr, a * 384,
                              [[3 * H, 128], [128 * 3 * H, 16], [1, 384]]),
                  inc=16)
                CK("wkvr%d_%d" % (i, a), "G")
                if a >= 2:
                    W("T", ("scan%d_%d" if warm else "z%d_%d") % (i, a - 2))
                W("T", "wkvr%d_%d" % (i, a))
                for h in range(HT):
                    if a == 0 and h % 4 == 0:
                        W("T", "amixg%d_%d" % (i, h // 4))
                    O("T", "matmul", out=pb[0 + pk][:, :], lhsT=kvrtile(pk, h, 0),
                      rhs=kin(h), start=(h == 0), stop=(h == HT - 1),
                      inc=(1 if h == HT - 1 else 0))
                CK("kmm%d_%d" % (i, a), "T")
                for h in range(HT):
                    O("T", "matmul", out=pb[2 + pk][:, :], lhsT=kvrtile(pk, h, 1),
                      rhs=vin(h), start=(h == 0), stop=(h == HT - 1),
                      inc=(1 if h == HT - 1 else 0))
                CK("vmm%d_%d" % (i, a), "T")
                if not warm:
                    for h in range(HT):
                        O("T", "matmul", out=pb[4 + pk][:, :],
                          lhsT=kvrtile(pk, h, 2), rhs=rin(h),
                          start=(h == 0), stop=(h == HT - 1),
                          inc=(1 if h == HT - 1 else 0))
                    CK("rmm%d_%d" % (i, a), "T")
                W("S", "kmm%d_%d" % (i, a))
                if a >= 1:
                    W("S", "scan%d_%d" % (i, a - 1))
                O("S", "activation", out=ekb, in_=pb[0 + pk][:, :],
                  func=AF.Exp, inc=1)
                CK("ekx%d_%d" % (i, a), "S")
                if warm:
                    W("S", "ekx%d_%d" % (i, a))
                    O("S", "mul", out=ekb, in_=ekb, mul=maskc, inc=1)
                CK("ek%d_%d" % (i, a), "S")
                if not warm:
                    W("S", "rmm%d_%d" % (i, a))
                    O("S", "activation", out=pb[4 + pk][:, :], in_=pb[4 + pk][:, :],
                      func=AF.Sigmoid, inc=1)
                    CK("sig%d_%d" % (i, a), "S")
                O("V", "memset", ewb, 1.0)
                O("V", "tensor_scalar", out=ewb, in0=ewb,
                  scalar1=ewc[:, a:a + 1], scalar2=None, op0=OP.mult)
                W("V", "ek%d_%d" % (i, a), "vmm%d_%d" % (i, a))
                O("V", "tensor_mul", out=ekv, in0=ekb,
                  in1=pb[2 + pk][:, :])
                O("V", "tensor_copy", out=ns[:, 0:1], in_=nstate[:, a:a + 1])
                O("V", "tensor_copy", out=ds[:, 0:1], in_=dstate[:, a:a + 1])
                O("V", "tensor_tensor_scan", out=ns[:, 1:513], data0=ewb,
                  data1=ekv, initial=nstate[:, a:a + 1], op0=OP.mult, op1=OP.add)
                O("V", "tensor_tensor_scan", out=ds[:, 1:513], data0=ewb,
                  data1=ekb, initial=dstate[:, a:a + 1],
                  op0=OP.mult, op1=OP.add)
                O("V", "tensor_copy", out=nstate[:, a:a + 1], in_=ns[:, 512:513])
                O("V", "tensor_copy", out=dstate[:, a:a + 1], in_=ds[:, 512:513],
                  inc=1)
                CK("scan%d_%d" % (i, a), "V")
                if warm:
                    continue
                W("S", "scan%d_%d" % (i, a))
                O("S", "activation", out=ekv, in_=ekb, func=AF.Copy,
                  scale=etfc[:, a:a + 1], inc=1)
                CK("ektf%d_%d" % (i, a), "S")
                W("V", "ektf%d_%d" % (i, a))
                O("V", "tensor_mul", out=ta, in0=ekv, in1=pb[2 + pk][:, :])
                O("V", "tensor_add", out=ta, in0=ta, in1=ns[:, 0:512])
                O("V", "tensor_add", out=ns[:, 0:512], in0=ekv, in1=ds[:, 0:512])
                O("V", "reciprocal", out=ns[:, 0:512], in_=ns[:, 0:512])
                W("V", "sig%d_%d" % (i, a))
                O("V", "tensor_mul", out=ta, in0=ta, in1=ns[:, 0:512])
                O("V", "tensor_mul", out=zt(a), in0=ta, in1=pb[4 + pk][:, :],
                  inc=1)
                CK("z%d_%d" % (i, a), "V")

            if warm:
                W("Q", "axn%d" % i)
                O("Q", "dma_start", out=xe_ap3(), in_=x_hbm_ap(i + 1), inc=16)
                CK("xl%d" % (i + 1), "Q")
                continue

            for h in range(HT):
                ph = h % 2
                if h >= 2:
                    W("G", "ymm%d_%d" % (i, h - 2))
                O("G", "dma_start", out=ybuf_ap(ph),
                  in_=bass.AP(cwo, h * H * 128,
                              [[128, 128], [128 * 128, 16], [1, 128]]), inc=16)
                CK("wy%d_%d" % (i, h), "G")
                if h == 0:
                    W("T", "z%d_15" % i)
                if h >= 2:
                    W("T", "ybc%d_%d" % (i, h - 2))
                W("T", "wy%d_%d" % (i, h))
                for a in range(HT):
                    O("T", "matmul", out=pb[0 + ph][:, :], lhsT=ybtile(ph, a),
                      rhs=zt(a), start=(a == 0), stop=(a == HT - 1),
                      inc=(1 if a == HT - 1 else 0))
                CK("ymm%d_%d" % (i, h), "T")
                W("S", "ymm%d_%d" % (i, h))
                O("S", "activation", out=ybt(h), in_=pb[0 + ph][:, :], func=AF.Copy,
                  inc=1)
                CK("ybc%d_%d" % (i, h), "S")

            for h in range(HT):
                W("S", "ybc%d_%d" % (i, h))
                if h >= 2:
                    W("S", "bsm%d_%d" % (i, h - 2))
                O("S", "activation", out=sqb[h % 2], in_=ybt(h), func=AF.Square,
                  inc=1)
                CK("bsq%d_%d" % (i, h), "S")
                W("T", "bsq%d_%d" % (i, h))
                O("T", "matmul", out=pb[6][0:1, :], lhsT=onesb[:, :], rhs=ybt(h),
                  start=(h == 0), stop=(h == HT - 1))
                O("T", "matmul", out=pb[7][0:1, :], lhsT=onesb[:, :],
                  rhs=sqb[h % 2], start=(h == 0), stop=(h == HT - 1), inc=1)
                CK("bsm%d_%d" % (i, h), "T")
            CK("bstat%d" % i, "T")
            W("V", "bstat%d" % i)
            O("V", "tensor_scalar", out=mu, in0=pb[6][0:1, :], scalar1=1.0 / H,
              scalar2=None, op0=OP.mult)
            O("V", "tensor_scalar", out=msq, in0=pb[7][0:1, :], scalar1=1.0 / H,
              scalar2=None, op0=OP.mult)
            O("V", "tensor_mul", out=tsm, in0=mu, in1=mu)
            O("V", "tensor_sub", out=msq, in0=msq, in1=tsm, inc=1)
            CK("bvar%d" % i, "V")
            W("S", "bvar%d" % i)
            O("S", "activation", out=rstd, in_=msq, func=AF.Sqrt, bias=constsb[0:1, 0:1], inc=1)
            CK("bsqr%d" % i, "S")
            W("V", "bsqr%d" % i)
            O("V", "reciprocal", out=rstd, in_=rstd, inc=1)
            CK("brstd%d" % i, "V")
            W("T", "brstd%d" % i)
            O("T", "matmul", out=pb[6][:, :], lhsT=onesrow[:, :], rhs=mu,
              start=True, stop=True)
            O("T", "matmul", out=pb[7][:, :], lhsT=onesrow[:, :], rhs=rstd,
              start=True, stop=True, inc=1)
            CK("bbc%d" % i, "T")
            W("V", "bbc%d" % i)
            for h in range(HT):
                O("V", "tensor_sub", out=ta, in0=ybt(h), in1=pb[6][:, :])
                O("V", "tensor_mul", out=ta, in0=ta, in1=pb[7][:, :])
                O("V", "tensor_add", out=xet(h), in0=xet(h), in1=ta,
                  inc=(1 if h == HT - 1 else 0))
            CK("x1%d" % i, "V")

            for h in range(HT):
                if h == 0:
                    W("S", "x1%d" % i)
                if h >= 2:
                    W("S", "csm%d_%d" % (i, h - 2))
                O("S", "activation", out=rin(h), in_=xet(h), func=AF.Copy)
                O("S", "activation", out=sqb[h % 2], in_=xet(h), func=AF.Square,
                  inc=1)
                CK("csq%d_%d" % (i, h), "S")
                if h == 0:
                    W("T", "x1%d" % i)
                W("T", "csq%d_%d" % (i, h))
                O("T", "matmul", out=pb[6][0:1, :], lhsT=onesb[:, :], rhs=rin(h),
                  start=(h == 0), stop=(h == HT - 1))
                O("T", "matmul", out=pb[7][0:1, :], lhsT=onesb[:, :],
                  rhs=sqb[h % 2], start=(h == 0), stop=(h == HT - 1), inc=1)
                CK("csm%d_%d" % (i, h), "T")
            CK("cstat%d" % i, "T")
            W("V", "cstat%d" % i)
            O("V", "tensor_scalar", out=mu, in0=pb[6][0:1, :], scalar1=1.0 / H,
              scalar2=None, op0=OP.mult)
            O("V", "tensor_scalar", out=msq, in0=pb[7][0:1, :], scalar1=1.0 / H,
              scalar2=None, op0=OP.mult)
            O("V", "tensor_mul", out=tsm, in0=mu, in1=mu)
            O("V", "tensor_sub", out=msq, in0=msq, in1=tsm, inc=1)
            CK("cvar%d" % i, "V")
            W("S", "cvar%d" % i)
            O("S", "activation", out=rstd, in_=msq, func=AF.Sqrt, bias=constsb[0:1, 0:1], inc=1)
            CK("csqr%d" % i, "S")
            W("V", "csqr%d" % i)
            O("V", "reciprocal", out=rstd, in_=rstd, inc=1)
            CK("crstd%d" % i, "V")
            W("T", "crstd%d" % i)
            O("T", "matmul", out=pb[6][:, :], lhsT=onesrow[:, :], rhs=mu,
              start=True, stop=True)
            O("T", "matmul", out=pb[7][:, :], lhsT=onesrow[:, :], rhs=rstd,
              start=True, stop=True, inc=1)
            CK("cbc%d" % i, "T")
            W("V", "cbc%d" % i)
            for g in range(4):
                for h in range(4 * g, 4 * g + 4):
                    O("V", "tensor_sub", out=ta, in0=xet(h), in1=pb[6][:, :])
                    O("V", "tensor_mul", out=x2n(h), in0=ta, in1=pb[7][:, :],
                      inc=(1 if h == 4 * g + 3 else 0))
                CK("x2ng%d_%d" % (i, g), "V")
            CK("x2n%d" % i, "V")

            for o in range(OT):
                po = o % 2
                if o < 2:
                    W("Q", ck_exp_done, "rmm%d_15" % i)
                else:
                    W("Q", "gumm%d_%d" % (i, o - 2))
                O("Q", "dma_start", out=gubuf_ap(po),
                  in_=bass.AP(cwgu, o * 256,
                              [[GUW, 128], [128 * GUW, 16], [1, 256]]), inc=16)
                CK("wgu%d_%d" % (i, o), "Q")
                if o >= 2:
                    W("T", "hact%d_%d" % (i, o - 2))
                W("T", "wgu%d_%d" % (i, o))
                for h in range(HT):
                    if o == 0 and h % 4 == 0:
                        W("T", "x2ng%d_%d" % (i, h // 4))
                    O("T", "matmul", out=pb[0 + po][:, :], lhsT=gutile(po, h, 0),
                      rhs=x2n(h), start=(h == 0), stop=(h == HT - 1),
                      inc=(1 if h == HT - 1 else 0))
                CK("gmm%d_%d" % (i, o), "T")
                for h in range(HT):
                    O("T", "matmul", out=pb[2 + po][:, :], lhsT=gutile(po, h, 1),
                      rhs=x2n(h), start=(h == 0), stop=(h == HT - 1),
                      inc=(1 if h == HT - 1 else 0))
                CK("gumm%d_%d" % (i, o), "T")
                W("S", "gmm%d_%d" % (i, o))
                O("S", "activation", out=sqb[o % 2], in_=pb[0 + po][:, :],
                  func=AF.Silu, inc=1)
                CK("silu%d_%d" % (i, o), "S")
                W("V", "silu%d_%d" % (i, o), "gumm%d_%d" % (i, o))
                O("V", "tensor_mul", out=hact(o), in0=sqb[o % 2],
                  in1=pb[2 + po][:, :], inc=1)
                CK("hact%d_%d" % (i, o), "V")

            for h in range(HT):
                ph = h % 2
                if h == 0:
                    W("G", "gumm%d_%d" % (i, OT - 1))
                if h >= 2:
                    W("G", "dmm%d_%d" % (i, h - 2))
                O("G", "dma_start", out=dbuf_ap(ph),
                  in_=bass.AP(cwd, h * 128, [[H, 128], [128 * H, 43], [1, 128]]),
                  inc=16)
                CK("wd%d_%d" % (i, h), "G")
                if h == 0:
                    W("T", "hact%d_42" % i)
                if h >= 2:
                    W("T", "y2c%d_%d" % (i, h - 2))
                W("T", "wd%d_%d" % (i, h))
                for o in range(OT):
                    O("T", "matmul", out=pb[4 + ph][:, :], lhsT=dtile(ph, o),
                      rhs=hact(o), start=(o == 0), stop=(o == OT - 1),
                      inc=(1 if o == OT - 1 else 0))
                CK("dmm%d_%d" % (i, h), "T")
                W("S", "dmm%d_%d" % (i, h))
                O("S", "activation", out=ybt(h), in_=pb[4 + ph][:, :],
                  func=AF.Copy, inc=1)
                CK("y2c%d_%d" % (i, h), "S")

            for h in range(HT):
                W("S", "y2c%d_%d" % (i, h))
                if h >= 2:
                    W("S", "dsm%d_%d" % (i, h - 2))
                O("S", "activation", out=sqb[h % 2], in_=ybt(h), func=AF.Square,
                  inc=1)
                CK("dsq%d_%d" % (i, h), "S")
                if h == 0:
                    W("T", "x2n%d" % i)
                W("T", "dsq%d_%d" % (i, h))
                O("T", "matmul", out=pb[6][0:1, :], lhsT=onesb[:, :], rhs=ybt(h),
                  start=(h == 0), stop=(h == HT - 1))
                O("T", "matmul", out=pb[7][0:1, :], lhsT=onesb[:, :],
                  rhs=sqb[h % 2], start=(h == 0), stop=(h == HT - 1), inc=1)
                CK("dsm%d_%d" % (i, h), "T")
            CK("dstat%d" % i, "T")
            W("V", "dstat%d" % i)
            O("V", "tensor_scalar", out=mu, in0=pb[6][0:1, :], scalar1=1.0 / H,
              scalar2=None, op0=OP.mult)
            O("V", "tensor_scalar", out=msq, in0=pb[7][0:1, :], scalar1=1.0 / H,
              scalar2=None, op0=OP.mult)
            O("V", "tensor_mul", out=tsm, in0=mu, in1=mu)
            O("V", "tensor_sub", out=msq, in0=msq, in1=tsm, inc=1)
            CK("dvar%d" % i, "V")
            W("S", "dvar%d" % i)
            O("S", "activation", out=rstd, in_=msq, func=AF.Sqrt, bias=constsb[0:1, 0:1], inc=1)
            CK("dsqr%d" % i, "S")
            W("V", "dsqr%d" % i)
            O("V", "reciprocal", out=rstd, in_=rstd, inc=1)
            CK("drstd%d" % i, "V")
            W("T", "drstd%d" % i)
            O("T", "matmul", out=pb[6][:, :], lhsT=onesrow[:, :], rhs=mu,
              start=True, stop=True)
            O("T", "matmul", out=pb[7][:, :], lhsT=onesrow[:, :], rhs=rstd,
              start=True, stop=True, inc=1)
            CK("dbc%d" % i, "T")
            W("V", "dbc%d" % i)
            for h in range(HT):
                O("V", "tensor_sub", out=ta, in0=ybt(h), in1=pb[6][:, :])
                O("V", "tensor_mul", out=ta, in0=ta, in1=pb[7][:, :])
                O("V", "tensor_add", out=xet(h), in0=xet(h), in1=ta,
                  inc=(1 if h == HT - 1 else 0))
            CK("outv%d" % i, "V")

            W("Q", "outv%d" % i)
            O("Q", "dma_start", out=y_hbm_ap(i - 1), in_=xe_ap3(), inc=16)
            CK("outd%d" % i, "Q")
            if i + 1 < NT:
                W("Q", "outd%d" % i)
                O("Q", "dma_start", out=xe_ap3(), in_=x_hbm_ap(i + 1), inc=16)
                CK("xl%d" % (i + 1), "Q")

        if ctx.mode == "dry":
            ctx.ck["END_Q"] = [("Q" + str(r), ctx.cnt["Q" + str(r)])
                               for r in range(NRING)
                               if ctx.cnt["Q" + str(r)] > 0]
        if ctx.mode == "Q":
            for src, val in ctx.ck["END_Q"]:
                ctx.handles["Q"].wait_ge(ctx.sems[src], val)

    dry = Ctx("dry")
    script(dry)

    with es:
        with ExitStack() as ses:
            sems = {}
            for k in "TVS":
                sems[k] = ses.enter_context(nc.semaphore("s" + k))
            for q in "QG":
                for r in range(NRING):
                    sems[q + str(r)] = ses.enter_context(
                        nc.semaphore("s%s%d" % (q, r)))
            block = ses.enter_context(nc.Block())

            @block.tensor
            def _(eng):
                script(Ctx("T", {"T": eng}, sems, dry.ck))

            @block.vector
            def _(eng):
                script(Ctx("V", {"V": eng}, sems, dry.ck))

            @block.scalar
            def _(eng):
                script(Ctx("S", {"S": eng}, sems, dry.ck))

            @block.sync
            def _(eng):
                script(Ctx("Q", {"Q": eng}, sems, dry.ck))

            @block.gpsimd
            def _(eng):
                script(Ctx("G", {"G": eng}, sems, dry.ck))

    return nc


def kernel(L0_attn_hidden, L0_attn_time_decay, L0_attn_time_first,
           L0_attn_time_mix_key, L0_attn_time_mix_value,
           L0_attn_time_mix_receptance, L0_attn_W_key, L0_attn_W_value,
           L0_attn_W_receptance, L0_attn_W_output, L0_ffn_lut,
           L0_ffn_gate_idx, L0_ffn_up_idx, L0_ffn_down_idx):
    B, T, _ = L0_attn_hidden.shape

    wkT = np.ascontiguousarray(np.asarray(L0_attn_W_key, np.float32).T)
    wvT = np.ascontiguousarray(np.asarray(L0_attn_W_value, np.float32).T)
    wrT = np.ascontiguousarray(np.asarray(L0_attn_W_receptance, np.float32).T)
    woT = np.ascontiguousarray(np.asarray(L0_attn_W_output, np.float32).T)

    g = np.asarray(L0_ffn_gate_idx, np.int32).T.reshape(H, OT, 128)
    u = np.asarray(L0_ffn_up_idx, np.int32).T.reshape(H, OT, 128)
    giuT = np.ascontiguousarray(np.stack([g, u], axis=2).reshape(H, GUW))
    didT = np.ascontiguousarray(np.asarray(L0_ffn_down_idx, np.int32).T)

    lutrow = np.ascontiguousarray(
        np.asarray(L0_ffn_lut, np.float32).reshape(1, 16))

    def colsify(v):
        return np.asarray(v, np.float32).reshape(HT, 128).T

    x_full = np.asarray(L0_attn_hidden, np.float32)

    in_maps = []
    for c in range(8):
        b, q = c // 4, c % 4
        t0 = q * TC
        xs = np.zeros((XW, H), np.float32)
        if q == 0:
            xs[512:] = x_full[b, 0:TC]
        else:
            xs[:] = x_full[b, t0 - 512:t0 + TC]
        x_fm = np.ascontiguousarray(xs.T)

        cols = np.zeros((128, 81), np.float32)
        cols[:, 0:16] = colsify(L0_attn_time_decay)
        cols[:, 16:32] = colsify(L0_attn_time_first)
        cols[:, 32:48] = colsify(np.asarray(L0_attn_time_mix_key).reshape(-1))
        cols[:, 48:64] = colsify(np.asarray(L0_attn_time_mix_value).reshape(-1))
        cols[:, 64:80] = colsify(
            np.asarray(L0_attn_time_mix_receptance).reshape(-1))
        cols[:, 80] = 0.0 if q == 0 else 1.0

        in_maps.append({
            "x": x_fm, "wkT": wkT, "wvT": wvT, "wrT": wrT, "woT": woT,
            "giuT": giuT, "didT": didT, "colsin": cols, "lutrow": lutrow,
        })

    nc = build_program()
    res = run_bass_kernel_spmd(nc, in_maps, core_ids=list(range(8)))

    out = np.empty((B, T, H), np.float32)
    for c in range(8):
        b, q = c // 4, c % 4
        out[b, q * TC:(q + 1) * TC, :] = res.results[c]["y"].T
    return out
